# version 14 (final)
# baseline (speedup 1.0000x reference)
import numpy as np
import ml_dtypes
import concourse.bass as bass
import concourse.mybir as mybir
from concourse.bass_utils import run_bass_kernel_spmd

F32 = mybir.dt.float32
BF16 = mybir.dt.bfloat16
AF = mybir.ActivationFunctionType
ALU = mybir.AluOpType
AX = mybir.AxisListType.X

T = 4096
D = 1024
CH = 512
NCH = T // CH
NB = CH // 128
H = 32
HD = 64
DS = 128
DIN = 2048
CONVD = 2304
DIP = 4384
DFF = 4096
AH = 16
NE = 8
NEG = -1.0e38
ATT_C = 30.0

ENGS = ("tensor", "vector", "scalar", "gpsimd", "sync")
STAGE = 3


class Buf:
    __slots__ = ("w", "rs", "name")

    def __init__(self, name=""):
        self.w = None
        self.rs = []
        self.name = name


class Prog:
    def __init__(self, nc):
        self.nc = nc
        self.cl = {e: [] for e in ENGS}
        self.cnt = {e: 0 for e in ENGS}
        self.sem = {e: nc.alloc_semaphore(f"sem_{e}") for e in ENGS}
        self.dsem = {e: nc.alloc_semaphore(f"dma_{e}")
                     for e in ("sync", "gpsimd", "scalar")}
        self.dcnt = {e: 0 for e in ("sync", "gpsimd", "scalar")}
        self.wm = {e: {} for e in ENGS}

    SELF_SYNC = ("vector", "scalar", "gpsimd")

    def _waits(self, e, deps):
        out = {}
        own = self.sem[e]
        skip_own = e not in self.SELF_SYNC
        for d in deps:
            if d is None:
                continue
            s, v = d
            if s is own and skip_own:
                continue
            if out.get(s, 0) < v:
                out[s] = v
        wm = self.wm[e]
        res = []
        for s, v in out.items():
            if wm.get(s, 0) >= v:
                continue
            wm[s] = v
            res.append((s, v))
        return res

    def op(self, e, fn, reads=(), writes=(), extra=()):
        deps = [b.w for b in reads]
        for b in writes:
            deps += b.rs
            deps.append(b.w)
        deps += list(extra)
        waits = self._waits(e, deps)
        self.cnt[e] += 1
        tok = (self.sem[e], self.cnt[e])
        sem = self.sem[e]

        def emit(eng, waits=waits, fn=fn, sem=sem):
            for s, v in waits:
                eng.wait_ge(s, v)
            fn(eng).then_inc(sem, 1)

        self.cl[e].append(emit)
        for b in reads:
            b.rs.append(tok)
        for b in writes:
            b.w = tok
            b.rs = []
        return tok

    def dma(self, q, out_ap, in_ap, reads=(), writes=(), extra=()):
        deps = [b.w for b in reads]
        for b in writes:
            deps += b.rs
            deps.append(b.w)
        deps += list(extra)
        waits = self._waits(q, deps)
        dsem = self.dsem[q]
        self.dcnt[q] += 1
        tok = (dsem, 16 * self.dcnt[q])

        def emit(eng, waits=waits, o=out_ap, i=in_ap, dsem=dsem):
            for s, v in waits:
                eng.wait_ge(s, v)
            eng.dma_start(o, i).then_inc(dsem, 16)

        self.cl[q].append(emit)
        for b in reads:
            b.rs.append(tok)
        for b in writes:
            b.w = tok
            b.rs = []
        return tok

    def fence(self):
        toks = [(self.sem[e], self.cnt[e]) for e in ENGS]
        toks += [(self.dsem[q], 16 * self.dcnt[q]) for q in
                 ("sync", "gpsimd", "scalar")]
        for e in ENGS:
            waits = self._waits(e, toks)
            if not waits:
                continue

            def emit(eng, waits=waits):
                for s, v in waits:
                    eng.wait_ge(s, v)

            self.cl[e].append(emit)

    def finish(self):
        for q in ("sync", "gpsimd", "scalar"):
            dsem = self.dsem[q]
            v = 16 * self.dcnt[q]
            if v == 0:
                continue

            def emit(eng, dsem=dsem, v=v):
                eng.wait_ge(dsem, v)

            self.cl["sync"].append(emit)

    def emit(self):
        nc = self.nc
        with nc.Block() as block:
            @block.tensor
            def _(t):
                for f in self.cl["tensor"]:
                    f(t)

            @block.vector
            def _(v):
                for f in self.cl["vector"]:
                    f(v)

            @block.scalar
            def _(s):
                for f in self.cl["scalar"]:
                    f(s)

            @block.gpsimd
            def _(g):
                for f in self.cl["gpsimd"]:
                    f(g)

            @block.sync
            def _(sy):
                for f in self.cl["sync"]:
                    f(sy)


class Region:

    def __init__(self, nc, off0, cap):
        self.nc = nc
        self.off0 = off0
        self.cap = cap
        self.cur = off0

    def reset(self):
        self.cur = off0 if False else self.off0

    def at(self, name, shape, dt):
        import math
        itm = {F32: 4, BF16: 2}[dt]
        nbytes = shape[1] * itm
        nbytes = (nbytes + 31) & ~31
        off = self.cur
        assert off + nbytes <= self.cap, (name, off, nbytes, self.cap)
        self.cur = off + nbytes
        return self.nc.alloc_sbuf_tensor_at(name, shape, dt, offset=off)


def build(nc, stage=STAGE, debug=()):
    P = Prog(nc)
    dbg_done = set()

    def DBG(name, ap, bufs):
        if name in debug and name not in dbg_done:
            dbg_done.add(name)
            shape = [ap.shape[i] for i in range(len(ap.shape))]
            t = nc.dram_tensor("dbg_" + name, shape, ap.dtype, kind="ExternalOutput")
            P.dma("sync", t.ap(), ap, reads=list(bufs))

    x_in = nc.dram_tensor("x", [T, D], F32, kind="ExternalInput")
    y_out = nc.dram_tensor("y", [T, D], F32, kind="ExternalOutput")

    def din(name, shape):
        return nc.dram_tensor(name, shape, F32, kind="ExternalInput")

    w_ln_w = din("attn_ln_w", [D])
    w_ln_b = din("attn_ln_b", [D])
    w_inproj = din("attn_in_proj_w", [D, DIP])
    w_conv = din("attn_conv_w", [CONVD, 4])
    w_convb = din("attn_conv_b", [CONVD])
    w_dtb = din("attn_dt_bias", [H])
    w_alog = din("attn_A_log", [H])
    w_dskip = din("attn_D_skip", [H])
    w_rms = din("attn_rms_w", [DIN])
    w_outp = din("attn_out_proj_w", [DIN, D])
    w_m1 = din("attn_mlp_w1", [D, DFF])
    w_mb1 = din("attn_mlp_b1", [DFF])
    w_m2 = din("attn_mlp_w2", [DFF, D])
    w_mb2 = din("attn_mlp_b2", [D])
    w_l1g = din("ffn_ln1_g", [D])
    w_l1b = din("ffn_ln1_b", [D])
    w_l2g = din("ffn_ln2_g", [D])
    w_l2b = din("ffn_ln2_b", [D])
    w_qkv = din("ffn_w_qkv", [D, 3 * D])
    w_wo = din("ffn_w_o", [D, D])
    w_gate = din("ffn_w_gate", [D, NE])
    w_e1 = din("ffn_w1", [NE, D, DFF])
    w_eb1 = din("ffn_b1", [NE, DFF])
    w_e2 = din("ffn_w2", [NE, DFF, D])
    w_eb2 = din("ffn_b2", [NE, D])

    x1_d = nc.dram_tensor("x1_scr", [T, D], F32, kind="Internal")
    q_d = nc.dram_tensor("q_scr", [8, 128, T], BF16, kind="Internal")
    xres_d = nc.dram_tensor("xres_scr", [T, D], F32, kind="Internal")
    u1_d = nc.dram_tensor("u1_scr", [T, D], F32, kind="Internal")
    b_x1d = Buf("x1_d")
    b_qd = Buf("q_d")
    b_xresd = Buf("xres_d")
    b_u1d = Buf("u1_d")

    lt_np = np.tril(np.ones((128, 128), dtype=np.float32)).T
    ident_bf = nc.inline_tensor(np.eye(128, dtype=ml_dtypes.bfloat16), "c_ident_bf")
    ident_f = nc.inline_tensor(np.eye(128, dtype=np.float32), "c_ident_f")
    lt_c = nc.inline_tensor(np.ascontiguousarray(lt_np), "c_lt")
    m1 = np.where(lt_np > 0, 0.0, NEG).astype(np.float32)
    mask4_c = nc.inline_tensor(np.ascontiguousarray(np.tile(m1, (1, 4))), "c_mask4")
    am = np.zeros((128, 512), dtype=np.float32)
    am[:, 0:128] = m1
    amask_c = nc.inline_tensor(np.ascontiguousarray(am), "c_amask")
    ones_f_c = nc.inline_tensor(np.ones((1, 512), dtype=np.float32), "c_ones_f")
    ones_b_c = nc.inline_tensor(np.ones((1, 512), dtype=ml_dtypes.bfloat16), "c_ones_b")

    def sb(name, shape, dt=F32):
        return nc.alloc_sbuf_tensor(name, shape, dt), Buf(name)

    id_bf, b_id_bf = sb("id_bf", [128, 128], BF16)
    id_f, b_id_f = sb("id_f", [128, 128], F32)
    lt_s, b_lt = sb("lt_s", [128, 128], F32)
    mask4_s, b_mask4 = sb("mask4_s", [128, 512], F32)
    ones_fs, b_ones_f = sb("ones_fs", [1, 512], F32)
    ones_bs, b_ones_b = sb("ones_bs", [1, 512], BF16)

    bc_a, b_bc_a = sb("bc_a", [128, 2048], F32)
    bc_rms, b_bc_rms = sb("bc_rms", [128, 2048], F32)

    dtb_s, b_dtb = sb("dtb_s", [H, 1], F32)
    alog_s, b_alog = sb("alog_s", [H, 1], F32)
    a_s, b_a = sb("a_s", [H, 1], F32)
    drow_s, b_drow = sb("drow_s", [1, H], F32)
    dcol_s, b_dcol = sb("dcol_s", [128, H], F32)
    convw_s, b_convw = sb("convw_s", [128, 18 * 4], F32)
    convb_s, b_convb = sb("convb_s", [128, 18], F32)
    lnrow_s, b_lnrow = sb("lnrow_s", [1, 512], F32)

    pb, B_pb = [], []
    for i in range(6):
        t = nc.alloc_psum_tensor(f"pb{i}", [128, 512], F32)
        pb.append(t)
        B_pb.append(Buf(f"pb{i}"))
    ptr = nc.alloc_psum_tensor("ptr", [128, 1024], BF16)
    b_ptr = Buf("ptr")
    b_ptrB = Buf("ptrB")
    PTRB = [b_ptr, b_ptrB]
    ptr2 = nc.alloc_psum_tensor("ptr2", [128, 512], F32)
    b_ptr2 = Buf("ptr2")

    x0_t, _ = sb("x0_t", [128, 2 * D], F32)
    B_x0 = [Buf("x0_0"), Buf("x0_1")]
    scr_t, b_scr = sb("scr_t", [128, D], F32)
    sp_view = scr_t[0:H, 0:CH]
    b_sp = b_scr
    scrb_t, b_scrb = sb("scrb_t", [128, DIN], BF16)
    b_scrbB = Buf("scrbB")
    SCRB = [b_scrb, b_scrbB]
    mb1_t, b_mb1 = sb("mb1_t", [128, 32], F32)
    mb2_t, b_mb2 = sb("mb2_t", [1, D], BF16)
    gw_s, b_gw = sb("gw_s", [128, 64], BF16)
    st_t, b_st = sb("st_t", [128, 48], F32)
    b_stB = Buf("stB")
    ST = [b_st, b_stB]
    rms_r, b_rmsr = sb("rms_r", [128, 4], F32)
    llrow_s, b_llrow = sb("llrow_s", [1, H], F32)
    llcol_s, b_llcol = sb("llcol_s", [H, 1], F32)
    tails_s, b_tails = sb("tails_s", [128, 18 * 3], F32)
    lfm_s, b_lfm = sb("lfm_s", [H, 128], F32)
    cfm_s, b_cfm2 = sb("cfm_s", [H, 128], F32)
    ctm_s, b_ctm = sb("ctm_s", [128, H], F32)
    expl_t, b_expl = sb("expl_t", [128, H], F32)
    ehat_t, b_ehat = sb("ehat_t", [128, H], F32)
    elast_t, b_elast = sb("elast_t", [128, H], F32)
    bmt_t, b_bmt = sb("bmt_t", [128, 128], BF16)
    b_bht = Buf("bht4")
    rl_s, b_rl = sb("rl_s", [1, 512], F32)

    off0 = nc.sbuf_base
    CAP = 229340
    reg = Region(nc, off0, CAP)
    xnfm_t = reg.at("xnfm_t", [128, 8 * CH], BF16)
    b_xnfm = Buf("xnfm")
    ipw_t = [reg.at(f"ipw{i}", [128, 8 * 128], BF16) for i in range(2)]
    B_ipw = [Buf("ipw0"), Buf("ipw1")]
    wst_t = [reg.at(f"wst{i}", [128, 512], BF16) for i in range(2)]
    B_wst = [Buf("wst0"), Buf("wst1")]
    hr_t = reg.at("hr_t", [128, 8 * CH], BF16)
    b_hr = Buf("hr")
    ovl0 = reg.cur

    sz_t = reg.at("sz_t", [128, NB * DIN], BF16)
    b_sz = Buf("sz")
    zw_t = [reg.at(f"zw{i}", [128, 4 * 512], BF16) for i in range(2)]
    B_zw = [Buf("zw0"), Buf("zw1")]
    raw_t = [reg.at(f"raw{i}", [128, 515], F32) for i in range(2)]
    B_raw = [Buf("raw0"), Buf("raw1")]
    cvo_t = reg.at("cvo_t", [128, 512], F32)
    b_cvo = Buf("cvo")
    xtm_t = reg.at("xtm_t", [128, NB * DIN], BF16)
    b_xtm = Buf("xtm")
    bmf_t = [reg.at(f"bmf_t{i}", [128, CH], BF16) for i in range(2)]
    B_bmf = [Buf("bmf0"), Buf("bmf1")]
    cmf_t = [reg.at(f"cmf_t{i}", [128, CH], BF16) for i in range(2)]
    B_cmf = [Buf("cmf0"), Buf("cmf1")]
    dt_fm = reg.at("dt_fm", [H, CH], F32)
    b_dtfm = Buf("dt_fm")
    dta_fm = reg.at("dta_fm", [H, CH], F32)
    b_dtafm = Buf("dta_fm")
    lndt_fm = [reg.at(f"lndt_fm{i}", [H, CH], F32) for i in range(2)]
    B_lndt = [Buf("lndt0"), Buf("lndt1")]
    dtt_all = [reg.at(f"dtt_all{i}", [128, NB * 2 * H], F32) for i in range(2)]
    B_dtt = [Buf("dtt0"), Buf("dtt1")]
    mlpfm_t = reg.at("mlpfm_t", [128, 8 * CH], BF16)
    b_mlpfm = Buf("mlpfm")
    u1q_t = reg.at("u1q_t", [128, D], F32)
    B_u1q = [Buf("u1q0"), Buf("u1q1")]
    bht4_t = reg.at("bht4_t", [128, 512], BF16)
    d2_t = [reg.at(f"d2_{i}", [2, 512], F32) for i in range(2)]
    B_d2 = [Buf("d2_0"), Buf("d2_1")]
    r2_t = [reg.at(f"r2_{i}", [2, 512], F32) for i in range(2)]
    B_r2 = [Buf("r2_0"), Buf("r2_1")]
    earg_t = reg.at("earg_t", [128, 512], F32)
    b_earg = Buf("earg")
    w4_t = [reg.at(f"w4_{i}", [128, 512], BF16) for i in range(2)]
    B_w4 = [Buf("w4_0"), Buf("w4_1")]
    h_st = reg.at("h_st", [128, H * HD], F32)
    b_h = Buf("h_st")
    h_bf = reg.at("h_bf", [128, H * HD], BF16)
    b_hbf = Buf("h_bf")
    yintra_t = reg.at("yintra_t", [128, 256], F32)
    b_yintra = Buf("yintra")
    y_blk = reg.at("y_blk", [128, DIN], F32)
    b_yblk = Buf("y_blk")
    ynfm_t = reg.at("ynfm_t", [128, 16 * CH], BF16)
    b_ynfm = Buf("ynfm")
    out1_t = reg.at("out1_t", [128, NB * D], F32)
    b_out1 = Buf("out1")
    p1_end = reg.cur
    opw_t = wst_t
    B_opw = B_wst
    mw2_t = wst_t
    B_mw2 = B_wst

    reg.cur = ovl0
    kfm_t = reg.at("kfm_t", [128, 8 * T], BF16)
    b_kfm = Buf("kfm")
    vsb_t = reg.at("vsb_t", [128, 32 * (16 * 65)], BF16)
    b_vsb = Buf("vsb")
    sub0 = reg.cur
    vw_t = reg.at("vw_t", [128, 4 * 512], BF16)
    b_vw = Buf("vw")
    p2_end = reg.cur
    reg.cur = sub0
    qsb_t = [reg.at(f"qsb{i}", [128, 512], BF16) for i in range(2)]
    B_qsb = [Buf("qsb0"), Buf("qsb1")]
    pt_t = [reg.at(f"pt{i}", [128, 512], BF16) for i in range(4)]
    B_pt = [Buf(f"pt{i}") for i in range(4)]
    rc_t = scr_t[:, 0:512]
    b_rc = b_scr
    p3_end = max(reg.cur, p2_end)

    reg.cur = ovl0
    acc_t = reg.at("acc_t", [128, 16 * D], F32)
    b_acc = Buf("acc")
    mfm_t = reg.at("mfm_t", [128, 8 * 2048], BF16)
    b_mfm = Buf("mfm")
    ew2_t = [reg.at(f"ew2_{i}", [128, 4 * D], BF16) for i in range(2)]
    B_ew2 = [[Buf(f"ew2_{i}_{k}") for k in range(2)] for i in range(2)]
    ew1_t = [reg.at(f"ew1_{i}", [128, 4 * 8 * 128], BF16) for i in range(2)]
    B_ew1 = [[Buf(f"ew1_{i}_{f}") for f in range(4)] for i in range(2)]
    gates_t = reg.at("gates_t", [128, 16 * NE], F32)
    b_gates = Buf("gates")
    mb1a_t = reg.at("mb1a_t", [128, 8 * 32], F32)
    b_mb1a = Buf("mb1a")
    mb2a_t = [reg.at(f"mb2a_{i}", [1, D], BF16) for i in range(2)]
    B_mb2a = [Buf("mb2a_0"), Buf("mb2a_1")]
    B_hrp = [Buf("hrp0"), Buf("hrp1")]
    p4_end = reg.cur

    def V(fn, reads=(), writes=(), extra=()):
        return P.op("vector", fn, reads, writes, extra)

    def S(fn, reads=(), writes=(), extra=()):
        return P.op("scalar", fn, reads, writes, extra)

    def TE(fn, reads=(), writes=(), extra=()):
        return P.op("tensor", fn, reads, writes, extra)

    def G(fn, reads=(), writes=(), extra=()):
        return P.op("gpsimd", fn, reads, writes, extra)

    P.dma("sync", id_bf[:], ident_bf.ap(), writes=[b_id_bf])
    P.dma("sync", id_f[:], ident_f.ap(), writes=[b_id_f])
    P.dma("sync", lt_s[:], lt_c.ap(), writes=[b_lt])
    P.dma("sync", mask4_s[:], mask4_c.ap(), writes=[b_mask4])
    P.dma("sync", ones_fs[:], ones_f_c.ap(), writes=[b_ones_f])
    P.dma("sync", ones_bs[:], ones_b_c.ap(), writes=[b_ones_b])
    P.dma("sync", dtb_s[:], w_dtb.ap().rearrange("(h o) -> h o", o=1), writes=[b_dtb])
    P.dma("sync", alog_s[:], w_alog.ap().rearrange("(h o) -> h o", o=1), writes=[b_alog])
    P.dma("sync", drow_s[:], w_dskip.ap().rearrange("(o h) -> o h", o=1), writes=[b_drow])
    P.dma("sync", convw_s[:].rearrange("p (c k) -> p c k", k=4),
          w_conv.ap().rearrange("(c p) k -> p c k", p=128), writes=[b_convw])

    S(lambda e: e.activation(a_s[:], alog_s[:], AF.Exp), reads=[b_alog], writes=[b_a])
    V(lambda e: e.tensor_scalar(a_s[:], a_s[:], -1.0, None, ALU.mult),
      reads=[b_a], writes=[b_a])

    def bcast_row(dst_ap, b_dst, src_ap, b_src, width):
        TE(lambda e: e.matmul(ptr2[:, 0:width], ones_fs[0:1, 0:128], src_ap,
                              start=True, stop=True),
           reads=[b_ones_f, b_src], writes=[b_ptr2])
        V(lambda e: e.tensor_copy(dst_ap, ptr2[:, 0:width]),
          reads=[b_ptr2], writes=[b_dst])

    def bcast_from_dram(dst_t, b_dst, src_dram, n, b0=0):
        for i in range(n):
            P.dma("sync", lnrow_s[:],
                  src_dram.ap().rearrange("(o d) -> o d", o=1)[0:1, b0 + i * 512:
                                                      b0 + (i + 1) * 512],
                  reads=[b_lnrow], writes=[b_lnrow])
            bcast_row(dst_t[:, i * 512:(i + 1) * 512], b_dst, lnrow_s[:],
                      b_lnrow, 512)

    bcast_from_dram(bc_a, b_bc_a, w_ln_w, 2)
    for i in range(2):
        P.dma("sync", lnrow_s[:],
              w_ln_b.ap().rearrange("(o d) -> o d", o=1)[0:1, i * 512:(i + 1) * 512],
              reads=[b_lnrow], writes=[b_lnrow])
        bcast_row(bc_a[:, D + i * 512:D + (i + 1) * 512], b_bc_a, lnrow_s[:],
                  b_lnrow, 512)
    bcast_from_dram(bc_rms, b_bc_rms, w_rms, 4)
    bcast_row(dcol_s[:], b_dcol, drow_s[:], b_drow, H)
    P.dma("sync", lfm_s[0:18, :], w_convb.ap().rearrange("(c p) -> c p", p=128),
          writes=[b_lfm])
    TE(lambda e: e.transpose(ptr2[:, 0:18], lfm_s[0:18, :], id_f[0:18, 0:18]),
       reads=[b_lfm, b_id_f], writes=[b_ptr2])
    V(lambda e: e.tensor_copy(convb_s[:], ptr2[:, 0:18]),
      reads=[b_ptr2], writes=[b_convb])
    P.dma("sync", lfm_s[:], w_mb1.ap().rearrange("(c p) -> c p", p=128),
          writes=[b_lfm])
    TE(lambda e: e.transpose(ptr2[:, 0:32], lfm_s[:], id_f[0:32, 0:32]),
       reads=[b_lfm, b_id_f], writes=[b_ptr2])
    V(lambda e: e.tensor_copy(mb1_t[:], ptr2[:, 0:32]),
      reads=[b_ptr2], writes=[b_mb1])
    P.dma("gpsimd", mb2_t[:], w_mb2.ap().rearrange("(o d) -> o d", o=1), writes=[b_mb2])
    for i in range(2):
        V(lambda e, i=i: e.memset(d2_t[i][0:1, :], 1.0), writes=[B_d2[i]])
        P.dma("sync", r2_t[i][1:2, :], ones_f_c.ap(), writes=[B_r2[i]])
    V(lambda e: e.memset(h_st[:], 0.0), writes=[b_h])
    G(lambda e: e.memset(h_bf[:], 0.0), writes=[b_hbf])
    V(lambda e: e.memset(tails_s[:], 0.0), writes=[b_tails])

    def ln_tile(src_ap, b_src, F, out_ap, b_out, affine=None, extra_reads=(),
                lane=0, out_bf=None):
        rd = [b_src] + list(extra_reads)
        bst = ST[lane]
        bsc = SCRB[lane]
        so = lane * 8
        sq = lane * 1024
        assert F <= 1024
        st = st_t[:, so:so + 8]
        V(lambda e: e.tensor_reduce(st[:, 0:1], src_ap, AX, ALU.add),
          reads=rd, writes=[bst])
        V(lambda e: e.scalar_tensor_tensor(scrb_t[:, sq:sq + F], src_ap, 1.0,
                                           src_ap, ALU.mult, ALU.mult,
                                           accum_out=st[:, 1:2]),
          reads=rd, writes=[bsc, bst])
        V(lambda e: e.tensor_scalar(st[:, 2:3], st[:, 0:1], 1.0 / F, None,
                                    ALU.mult),
          reads=[bst], writes=[bst])
        V(lambda e: e.tensor_tensor(st[:, 3:4], st[:, 2:3], st[:, 2:3],
                                    ALU.mult),
          reads=[bst], writes=[bst])
        V(lambda e: e.scalar_tensor_tensor(st[:, 3:4], st[:, 1:2], 1.0 / F,
                                           st[:, 3:4], ALU.mult, ALU.subtract),
          reads=[bst], writes=[bst])
        V(lambda e: e.tensor_scalar(st[:, 3:4], st[:, 3:4], 1e-5, None,
                                    ALU.add),
          reads=[bst], writes=[bst])
        S(lambda e: e.activation(st[:, 4:5], st[:, 3:4], AF.Sqrt),
          reads=[bst], writes=[bst])
        V(lambda e: e.reciprocal(st[:, 4:5], st[:, 4:5]),
          reads=[bst], writes=[bst])
        V(lambda e: e.scalar_tensor_tensor(st[:, 5:6], st[:, 2:3], -1.0,
                                           st[:, 4:5], ALU.mult, ALU.mult),
          reads=[bst], writes=[bst])
        V(lambda e: e.tensor_scalar(out_ap, src_ap, st[:, 4:5],
                                    st[:, 5:6], ALU.mult, ALU.add),
          reads=rd + [bst], writes=[b_out])
        if affine is not None:
            wap, bap, b_wb = affine
            V(lambda e: e.tensor_tensor(out_ap, out_ap, wap, ALU.mult),
              reads=[b_wb, b_out], writes=[b_out])
            if out_bf is not None:
                bfap, b_bf = out_bf
                V(lambda e: e.tensor_tensor(bfap, out_ap, bap, ALU.add),
                  reads=[b_wb, b_out], writes=[b_bf])
            else:
                V(lambda e: e.tensor_tensor(out_ap, out_ap, bap, ALU.add),
                  reads=[b_wb, b_out], writes=[b_out])

    def cast_group(src_ap, b_src, dst_t, b_dst, c0, cstride, d0, g, lane,
                   src_is_bf=False):
        bpt = PTRB[lane]
        po = lane * 512
        if src_is_bf:
            stage_ap, b_stage = src_ap, b_src
            soff = d0 * 128
        else:
            bsc = SCRB[lane]
            sq = lane * 1024
            V(lambda e, d0=d0, g=g: e.tensor_copy(
                scrb_t[:, sq:sq + g * 128],
                src_ap[:, d0 * 128:(d0 + g) * 128]),
              reads=[b_src], writes=[bsc])
            stage_ap, b_stage = scrb_t, bsc
            soff = sq
        for d in range(g):
            TE(lambda e, d=d: e.transpose(
                ptr[:, po + d * 128:po + (d + 1) * 128],
                stage_ap[:, soff + d * 128:soff + (d + 1) * 128], id_bf[:]),
               reads=[b_stage, b_id_bf], writes=[bpt])
        src_r = ptr[:, po:po + g * 128].rearrange("p (a c) -> p a c", c=128)
        dst_r = dst_t[:].rearrange("p (a c) -> p a c", c=cstride)[
            :, (c0 // cstride) + d0:(c0 // cstride) + d0 + g,
            (c0 % cstride):(c0 % cstride) + 128]
        V(lambda e, s=src_r, dd=dst_r: e.tensor_copy(dd, s),
          reads=[bpt], writes=[b_dst])

    def cast_transpose(src_ap, b_src, n, dst_t, b_dst, c0, cstride, lane=0,
                       src_is_bf=False):
        for d0 in range(0, n, 4):
            g = min(4, n - d0)
            cast_group(src_ap, b_src, dst_t, b_dst, c0, cstride, d0, g, lane,
                       src_is_bf)

    OPB = (0, 1, 4, 5)

    def fm_rhs(d):
        return xnfm_t[:, d * CH:(d + 1) * CH]

    def prep_units(ch):
        par = ch % 2
        t0 = ch * CH
        units = []

        def a_unit(t):
            def u(t=t):
                xb = B_x0[t % 2]
                P.dma("sync", x0_t[:, (t % 2) * D:(t % 2 + 1) * D],
                      x_in.ap()[t0 + t * 128: t0 + (t + 1) * 128, :],
                      writes=[xb])
                src = x0_t[:, (t % 2) * D:(t % 2 + 1) * D]
                ln_tile(src, xb, D, scr_t[:, 0:D], b_scr, lane=1)
                P.dma("sync", u1_d.ap()[t0 + t * 128: t0 + (t + 1) * 128, :],
                      scr_t[:, 0:D], reads=[b_scr], writes=[b_u1d])
                ln_tile(scr_t[:, 0:D], b_scr, D, scr_t[:, 0:D], b_scr,
                        affine=(bc_a[:, 0:D], bc_a[:, D:2 * D], b_bc_a),
                        lane=1, out_bf=(scrb_t[:, 1024:2048], b_scrbB))
                cast_transpose(scrb_t[:, 1024:2048], b_scrbB, 8, xnfm_t,
                               b_xnfm, t * 128, CH, lane=1, src_is_bf=True)
            return u

        for t in range(NB):
            units.append(a_unit(t))

        def ft_unit(ft):
            def u(ft=ft):
                fw = 128 if ft < 18 else 32
                c0 = 2048 + ft * 128
                wb = B_ipw[ft % 2]
                srcw = w_inproj.ap()[:, c0:c0 + fw].rearrange(
                    "(dc p) f -> p dc f", p=128)
                P.dma("gpsimd",
                      ipw_t[ft % 2][:, 0:8 * fw].rearrange(
                          "p (dc f) -> p dc f", f=fw),
                      srcw, writes=[wb])
                bi = 2 + ft % 2
                bank = B_pb[bi]
                for d in range(8):
                    TE(lambda e, ft=ft, d=d, fw=fw, bi=bi: e.matmul(
                        pb[bi][0:fw, 0:CH],
                        ipw_t[ft % 2][:, d * fw:(d + 1) * fw], fm_rhs(d),
                        start=(d == 0), stop=(d == 7)),
                       reads=[wb, b_xnfm], writes=[bank])
                if ft < 18:
                    rb = B_raw[ft % 2]
                    V(lambda e, ft=ft, bi=bi: e.tensor_copy(
                        raw_t[ft % 2][:, 3:3 + CH], pb[bi][:, 0:CH]),
                      reads=[bank], writes=[rb])
                    G(lambda e, ft=ft: e.tensor_copy(
                        raw_t[ft % 2][:, 0:3],
                        tails_s[:, ft * 3:(ft + 1) * 3]),
                      reads=[b_tails], writes=[rb])
                    G(lambda e, ft=ft: e.tensor_copy(
                        tails_s[:, ft * 3:(ft + 1) * 3],
                        raw_t[ft % 2][:, CH:CH + 3]),
                      reads=[rb], writes=[b_tails])
                    wc = convw_s
                    V(lambda e, ft=ft: e.tensor_scalar(
                        cvo_t[:], raw_t[ft % 2][:, 0:CH],
                        wc[:, ft * 4:ft * 4 + 1], convb_s[:, ft:ft + 1],
                        ALU.mult, ALU.add),
                      reads=[rb, b_convw, b_convb], writes=[b_cvo])
                    for k in range(1, 4):
                        V(lambda e, ft=ft, k=k: e.scalar_tensor_tensor(
                            cvo_t[:], raw_t[ft % 2][:, k:k + CH],
                            wc[:, ft * 4 + k:ft * 4 + k + 1], cvo_t[:],
                            ALU.mult, ALU.add),
                          reads=[rb, b_convw, b_cvo], writes=[b_cvo])
                    S(lambda e: e.activation(cvo_t[:], cvo_t[:], AF.Silu),
                      reads=[b_cvo], writes=[b_cvo])
                    if ft < 16:
                        cast_transpose(cvo_t[:], b_cvo, 4, xtm_t, b_xtm,
                                       ft * 128, DIN, lane=1)
                    elif ft == 16:
                        V(lambda e: e.tensor_copy(bmf_t[par][:], cvo_t[:]),
                          reads=[b_cvo], writes=[B_bmf[par]])
                    else:
                        V(lambda e: e.tensor_copy(cmf_t[par][:], cvo_t[:]),
                          reads=[b_cvo], writes=[B_cmf[par]])
                else:
                    V(lambda e, bi=bi: e.tensor_scalar(
                        sp_view, pb[bi][0:H, 0:CH], dtb_s[:], None, ALU.add),
                      reads=[bank, b_dtb], writes=[b_sp])
                    S(lambda e: e.activation(dt_fm[:], sp_view, AF.Abs),
                      reads=[b_sp], writes=[b_dtfm])
                    S(lambda e: e.activation(dt_fm[:], dt_fm[:], AF.Exp,
                                             scale=-1.0),
                      reads=[b_dtfm], writes=[b_dtfm])
                    V(lambda e: e.tensor_scalar(dt_fm[:], dt_fm[:], 1.0, None,
                                                ALU.add),
                      reads=[b_dtfm], writes=[b_dtfm])
                    S(lambda e: e.activation(dt_fm[:], dt_fm[:], AF.Ln),
                      reads=[b_dtfm], writes=[b_dtfm])
                    V(lambda e: e.tensor_scalar(sp_view, sp_view, 0.0, None,
                                                ALU.max),
                      reads=[b_sp], writes=[b_sp])
                    V(lambda e: e.tensor_tensor(dt_fm[:], dt_fm[:], sp_view,
                                                ALU.add),
                      reads=[b_sp, b_dtfm], writes=[b_dtfm])
                    V(lambda e: e.tensor_scalar(dta_fm[:], dt_fm[:], a_s[:],
                                                None, ALU.mult),
                      reads=[b_dtfm, b_a], writes=[b_dtafm])
                    S(lambda e: e.activation(lndt_fm[par][:], dt_fm[:], AF.Ln),
                      reads=[b_dtfm], writes=[B_lndt[par]])
            return u

        for ft in range(19):
            units.append(ft_unit(ft))

        def dtt_unit():
            def u():
                for b in range(NB):
                    for j, (srct, bsrc) in enumerate(
                            ((dta_fm, b_dtafm), (lndt_fm[par], B_lndt[par]))):
                        pc = ((2 * b + j) % 2) * H
                        TE(lambda e, b=b, srct=srct, bsrc=bsrc, pc=pc:
                           e.transpose(ptr2[:, pc:pc + H],
                                       srct[:, b * 128:(b + 1) * 128],
                                       id_f[0:H, 0:H]),
                           reads=[bsrc, b_id_f], writes=[b_ptr2])
                        V(lambda e, b=b, j=j, pc=pc: e.tensor_copy(
                            dtt_all[par][:, b * 2 * H + j * H:
                                         b * 2 * H + (j + 1) * H],
                            ptr2[:, pc:pc + H]),
                          reads=[b_ptr2], writes=[B_dtt[par]])
            return u

        units.append(dtt_unit())
        return units

    def z_part(ch):
        for zg in range(4):
            for hf in range(2):
                zb = B_zw[hf]
                srcw = w_inproj.ap()[hf * 512:(hf + 1) * 512,
                                     zg * 512:(zg + 1) * 512].rearrange(
                    "(dc p) f -> p dc f", p=128)
                P.dma("gpsimd",
                      zw_t[hf][:].rearrange("p (dc f) -> p dc f", f=512),
                      srcw, writes=[zb])
            for t in range(NB):
                bi = 2 + t % 2
                bank = B_pb[bi]
                for d in range(8):
                    zb = B_zw[d // 4]
                    TE(lambda e, t=t, d=d, bi=bi: e.matmul(
                        pb[bi][:, 0:512],
                        xnfm_t[:, d * CH + t * 128: d * CH + (t + 1) * 128],
                        zw_t[d // 4][:, (d % 4) * 512:(d % 4 + 1) * 512],
                        start=(d == 0), stop=(d == 7)),
                       reads=[zb, b_xnfm], writes=[bank])
                S(lambda e, zg=zg, t=t, bi=bi: e.activation(
                    sz_t[:, t * DIN + zg * 512: t * DIN + (zg + 1) * 512],
                    pb[bi][:, 0:512], AF.Silu),
                  reads=[bank], writes=[b_sz])

    def scan_block(ch, b, pend=None):
        par = ch % 2
        dtaT = dtt_all[par][:, b * 2 * H: b * 2 * H + H]
        lndtT = dtt_all[par][:, b * 2 * H + H: (b + 1) * 2 * H]
        bdt = B_dtt[par]
        bbmf = B_bmf[par]
        bcmf = B_cmf[par]
        TE(lambda e: e.matmul(ptr2[0:H, 0:128], dtaT, lt_s[:],
                              start=True, stop=True),
           reads=[bdt, b_lt], writes=[b_ptr2])
        V(lambda e: e.tensor_copy(lfm_s[:], ptr2[0:H, 0:128]),
          reads=[b_ptr2], writes=[b_lfm])
        TE(lambda e: e.matmul(ptr2[:, 0:H], lt_s[:], dtaT,
                              start=True, stop=True),
           reads=[bdt, b_lt], writes=[b_ptr2])
        S(lambda e: e.activation(expl_t[:], ptr2[:, 0:H], AF.Exp),
          reads=[b_ptr2], writes=[b_expl])
        V(lambda e: e.tensor_tensor(ctm_s[:], lndtT, ptr2[:, 0:H],
                                    ALU.subtract),
          reads=[bdt, b_ptr2], writes=[b_ctm])
        V(lambda e, b=b: e.tensor_tensor(
            cfm_s[:], lndt_fm[par][:, b * 128:(b + 1) * 128], lfm_s[:],
            ALU.subtract),
          reads=[B_lndt[par], b_lfm], writes=[b_cfm2])
        P.dma("sync", llrow_s[:], lfm_s[0:H, 127:128], reads=[b_lfm],
              writes=[b_llrow])
        TE(lambda e: e.matmul(ptr2[:, 0:H], ones_fs[0:1, 0:128], llrow_s[:],
                              start=True, stop=True),
           reads=[b_ones_f, b_llrow], writes=[b_ptr2])
        S(lambda e: e.activation(elast_t[:], ptr2[:, 0:H], AF.Exp),
          reads=[b_ptr2], writes=[b_elast])
        V(lambda e: e.tensor_tensor(ehat_t[:], ptr2[:, 0:H], ctm_s[:],
                                    ALU.add),
          reads=[b_ptr2, b_ctm], writes=[b_ehat])
        S(lambda e: e.activation(ehat_t[:], ehat_t[:], AF.Exp),
          reads=[b_ehat], writes=[b_ehat])
        TE(lambda e, b=b: e.transpose(ptr[:, 0:128],
                                      bmf_t[par][:, b * 128:(b + 1) * 128],
                                      id_bf[:]),
           reads=[bbmf, b_id_bf], writes=[b_ptr])
        V(lambda e: e.tensor_copy(bmt_t[:], ptr[:, 0:128]),
          reads=[b_ptr], writes=[b_bmt])
        for i in range(4):
            TE(lambda e, b=b, i=i: e.matmul(
                pb[2][:, i * 128:(i + 1) * 128],
                bmf_t[par][:, b * 128:(b + 1) * 128],
                cmf_t[par][:, b * 128:(b + 1) * 128],
                start=True, stop=True),
               reads=[bbmf, bcmf], writes=[B_pb[2]])

        def emit_difg(g):
            gb = g % 2
            P.dma("sync", r2_t[gb][0:1, :], lfm_s[4 * g:4 * g + 4, :],
                  reads=[b_lfm], writes=[B_r2[gb]])
            P.dma("sync", d2_t[gb][1:2, :], cfm_s[4 * g:4 * g + 4, :],
                  reads=[b_cfm2], writes=[B_d2[gb]])
            for i in range(4):
                TE(lambda e, gb=gb, i=i: e.matmul(
                    pb[gb][:, i * 128:(i + 1) * 128],
                    d2_t[gb][:, i * 128:(i + 1) * 128],
                    r2_t[gb][:, i * 128:(i + 1) * 128],
                    start=True, stop=True),
                   reads=[B_d2[gb], B_r2[gb]], writes=[B_pb[gb]])

        def emit_w(g):
            gb = g % 2
            V(lambda e, gb=gb: e.tensor_tensor(earg_t[:], pb[gb][:],
                                               mask4_s[:], ALU.add),
              reads=[B_pb[gb], b_mask4], writes=[b_earg])
            S(lambda e: e.activation(earg_t[:], earg_t[:], AF.Exp),
              reads=[b_earg], writes=[b_earg])
            V(lambda e: e.tensor_tensor(w4_t[gb][:], earg_t[:],
                                        pb[2][:], ALU.mult),
              reads=[b_earg, B_pb[2]], writes=[B_w4[gb]])

        def emit_ys(g):
            gb = g % 2
            for i in range(4):
                h = 4 * g + i
                TE(lambda e, b=b, gb=gb, h=h, i=i: e.matmul(
                    pb[4][:, gb * 256 + i * 64: gb * 256 + (i + 1) * 64],
                    w4_t[gb][:, i * 128:(i + 1) * 128],
                    xtm_t[:, b * DIN + h * HD: b * DIN + (h + 1) * HD],
                    start=True, stop=True),
                   reads=[B_w4[gb], b_xtm], writes=[B_pb[4]])
                TE(lambda e, b=b, gb=gb, h=h, i=i: e.matmul(
                    pb[5][:, gb * 256 + i * 64: gb * 256 + (i + 1) * 64],
                    cmf_t[par][:, b * 128:(b + 1) * 128],
                    h_bf[:, h * HD:(h + 1) * HD],
                    start=True, stop=True),
                   reads=[bcmf, b_hbf], writes=[B_pb[5]])
            for i in range(4):
                h = 4 * g + i
                V(lambda e, h=h, i=i: e.tensor_scalar(
                    bht4_t[:, i * 128:(i + 1) * 128],
                    bmt_t[:], ehat_t[:, h:h + 1], None, ALU.mult),
                  reads=[b_bmt, b_ehat], writes=[b_bht])
            for i in range(4):
                h = 4 * g + i
                TE(lambda e, b=b, gb=gb, h=h, i=i: e.matmul(
                    ptr2[:, gb * 256 + i * 64: gb * 256 + (i + 1) * 64],
                    bht4_t[:, i * 128:(i + 1) * 128],
                    xtm_t[:, b * DIN + h * HD: b * DIN + (h + 1) * HD],
                    start=True, stop=True),
                   reads=[b_bht, b_xtm], writes=[b_ptr2])

        def emit_epi(g):
            gb = g % 2
            V(lambda e, gb=gb: e.tensor_copy(
                yintra_t[:], pb[4][:, gb * 256:(gb + 1) * 256]),
              reads=[B_pb[4]], writes=[b_yintra])
            for i in range(4):
                h = 4 * g + i
                V(lambda e, gb=gb, h=h, i=i: e.scalar_tensor_tensor(
                    y_blk[:, h * HD:(h + 1) * HD],
                    pb[5][:, gb * 256 + i * 64: gb * 256 + (i + 1) * 64],
                    expl_t[:, h:h + 1],
                    yintra_t[:, i * 64:(i + 1) * 64],
                    ALU.mult, ALU.add),
                  reads=[B_pb[5], b_expl, b_yintra], writes=[b_yblk])
                V(lambda e, b=b, h=h: e.scalar_tensor_tensor(
                    y_blk[:, h * HD:(h + 1) * HD],
                    xtm_t[:, b * DIN + h * HD: b * DIN + (h + 1) * HD],
                    dcol_s[:, h:h + 1],
                    y_blk[:, h * HD:(h + 1) * HD],
                    ALU.mult, ALU.add),
                  reads=[b_xtm, b_dcol, b_yblk], writes=[b_yblk])
                V(lambda e, gb=gb, h=h, i=i: e.scalar_tensor_tensor(
                    h_st[:, h * HD:(h + 1) * HD],
                    h_st[:, h * HD:(h + 1) * HD],
                    elast_t[:, h:h + 1],
                    ptr2[:, gb * 256 + i * 64: gb * 256 + (i + 1) * 64],
                    ALU.mult, ALU.add),
                  reads=[b_elast, b_ptr2], writes=[b_h])

        emit_difg(0)
        emit_w(0)
        if pend is not None:
            pend[0]()
        emit_difg(1)
        for g in range(8):
            emit_ys(g)
            if g + 1 < 8:
                emit_w(g + 1)
            if g + 2 < 8:
                emit_difg(g + 2)
            if pend is not None and g % 2 == 0:
                pend[1][g // 2]()
            emit_epi(g)
        G(lambda e: e.tensor_copy(h_bf[:], h_st[:]), reads=[b_h],
          writes=[b_hbf])


        def chain(b=b):
            G(lambda e: e.tensor_tensor(y_blk[:], y_blk[:],
                                        sz_t[:, b * DIN:(b + 1) * DIN],
                                        ALU.mult),
              reads=[b_sz, b_yblk], writes=[b_yblk])
            V(lambda e: e.scalar_tensor_tensor(scrb_t[:], y_blk[:], 1.0,
                                               y_blk[:], ALU.mult, ALU.mult,
                                               accum_out=rms_r[:, 0:1]),
              reads=[b_yblk], writes=[b_scrb, b_scrbB, b_rmsr])
            V(lambda e: e.tensor_scalar(rms_r[:, 1:2], rms_r[:, 0:1],
                                        1.0 / DIN, 1e-5, ALU.mult, ALU.add),
              reads=[b_rmsr], writes=[b_rmsr])
            S(lambda e: e.activation(rms_r[:, 1:2], rms_r[:, 1:2], AF.Ln),
              reads=[b_rmsr], writes=[b_rmsr])
            V(lambda e: e.tensor_scalar(rms_r[:, 1:2], rms_r[:, 1:2], -0.5,
                                        None, ALU.mult),
              reads=[b_rmsr], writes=[b_rmsr])
            S(lambda e: e.activation(rms_r[:, 1:2], rms_r[:, 1:2], AF.Exp),
              reads=[b_rmsr], writes=[b_rmsr])
            V(lambda e: e.tensor_scalar(y_blk[:], y_blk[:], rms_r[:, 1:2],
                                        None, ALU.mult),
              reads=[b_rmsr, b_yblk], writes=[b_yblk])
            G(lambda e: e.tensor_tensor(y_blk[:], y_blk[:], bc_rms[:],
                                        ALU.mult),
              reads=[b_bc_rms, b_yblk], writes=[b_yblk])

        casts = []
        for grp in range(4):
            def cg(grp=grp, b=b):
                cast_group(y_blk[:], b_yblk, ynfm_t, b_ynfm, b * 128, CH,
                           grp * 4, 4, 0)
            casts.append(cg)
        return (chain, casts)

    def post_units(ch):
        t0 = ch * CH
        units = []

        def op_half_unit(half):
            def u(half=half):
                for t in range(2):
                    P.dma("sync", u1q_t[:, (t % 2) * 512:(t % 2 + 1) * 512],
                          u1_d.ap()[t0 + t * 128: t0 + (t + 1) * 128,
                                    half * 512:(half + 1) * 512],
                          reads=[b_u1d], writes=[B_u1q[t % 2]])
                for k in range(16):
                    wb = B_opw[k % 2]
                    src = w_outp.ap()[k * 128:(k + 1) * 128,
                                      half * 512:(half + 1) * 512]
                    P.dma("gpsimd", opw_t[k % 2][:], src, writes=[wb])
                    for t in range(NB):
                        bi = OPB[t]
                        TE(lambda e, k=k, t=t, bi=bi: e.matmul(
                            pb[bi][:, 0:512],
                            ynfm_t[:, k * CH + t * 128: k * CH + (t + 1) * 128],
                            opw_t[k % 2][:],
                            start=(k == 0), stop=(k == 15)),
                           reads=[wb, b_ynfm], writes=[B_pb[bi]])
                for t in range(NB):
                    bi = OPB[t]
                    V(lambda e, t=t, half=half, bi=bi: e.tensor_tensor(
                        out1_t[:, t * D + half * 512: t * D + (half + 1) * 512],
                        pb[bi][:, 0:512],
                        u1q_t[:, (t % 2) * 512:(t % 2 + 1) * 512],
                        ALU.add),
                      reads=[B_pb[bi], B_u1q[t % 2]], writes=[b_out1])
                    if t + 2 < NB:
                        P.dma("sync",
                              u1q_t[:, (t % 2) * 512:(t % 2 + 1) * 512],
                              u1_d.ap()[t0 + (t + 2) * 128:
                                        t0 + (t + 3) * 128,
                                        half * 512:(half + 1) * 512],
                              reads=[b_u1d], writes=[B_u1q[t % 2]])
            return u

        units.append(op_half_unit(0))
        units.append(op_half_unit(1))

        def mlp_ln_unit(t):
            def u(t=t):
                ln_tile(out1_t[:, t * D:(t + 1) * D], b_out1, D,
                        y_blk[:, 0:D], b_yblk,
                        affine=(bc_a[:, 0:D], bc_a[:, D:2 * D], b_bc_a),
                        lane=0, out_bf=(scrb_t[:, 0:1024], b_scrb))
                cast_transpose(scrb_t[:, 0:1024], b_scrb, 8, mlpfm_t, b_mlpfm,
                               t * 128, CH, lane=0, src_is_bf=True)
            return u

        for t in range(NB):
            units.append(mlp_ln_unit(t))

        def mlp_fc_unit(fc):
            def u(fc=fc):
                c0 = fc * 1024
                for f in range(8):
                    fb = B_zw[f % 2]
                    src1 = w_m1.ap()[:, c0 + f * 128:
                                     c0 + (f + 1) * 128].rearrange(
                        "(dc p) f -> p dc f", p=128)
                    P.dma("gpsimd",
                          zw_t[f % 2][:, 0:1024].rearrange(
                              "p (dc f) -> p dc f", f=128),
                          src1, writes=[fb])
                    bi = 4 + f % 2
                    bank = B_pb[bi]
                    for d in range(8):
                        TE(lambda e, f=f, d=d, bi=bi: e.matmul(
                            pb[bi][:, 0:512],
                            zw_t[f % 2][:, d * 128:(d + 1) * 128],
                            mlpfm_t[:, d * CH:(d + 1) * CH],
                            start=(d == 0), stop=(d == 7)),
                           reads=[fb, b_mlpfm], writes=[bank])
                    idx = fc * 8 + f
                    S(lambda e, f=f, idx=idx, bi=bi: e.activation(
                        hr_t[:, f * 512:(f + 1) * 512], pb[bi][:, 0:512],
                        AF.Relu, bias=mb1_t[:, idx:idx + 1]),
                      reads=[bank, b_mb1], writes=[b_hr])
                last = (fc == 3)
                for half in range(2):
                    for k in range(8):
                        w2b = B_mw2[k % 2]
                        src2 = w_m2.ap()[c0 + k * 128:c0 + (k + 1) * 128,
                                         half * 512:(half + 1) * 512]
                        P.dma("gpsimd", mw2_t[k % 2][:], src2, writes=[w2b])
                        for t in range(NB):
                            bi = OPB[t]
                            TE(lambda e, k=k, t=t, bi=bi: e.matmul(
                                pb[bi][:, 0:512],
                                hr_t[:, k * 512 + t * 128:
                                     k * 512 + (t + 1) * 128],
                                mw2_t[k % 2][:],
                                start=(k == 0), stop=(k == 7 and not last)),
                               reads=[w2b, b_hr], writes=[B_pb[bi]])
                    if last:
                        for t in range(NB):
                            bi = OPB[t]
                            TE(lambda e, t=t, half=half, bi=bi: e.matmul(
                                pb[bi][:, 0:512], ones_bs[0:1, 0:128],
                                mb2_t[0:1, half * 512:(half + 1) * 512],
                                start=False, stop=True),
                               reads=[b_ones_b, b_mb2], writes=[B_pb[bi]])
                    for t in range(NB):
                        bi = OPB[t]
                        V(lambda e, t=t, half=half, bi=bi: e.tensor_tensor(
                            out1_t[:, t * D + half * 512:
                                   t * D + (half + 1) * 512],
                            pb[bi][:, 0:512],
                            out1_t[:, t * D + half * 512:
                                   t * D + (half + 1) * 512],
                            ALU.add),
                          reads=[B_pb[bi], b_out1], writes=[b_out1])
            return u

        for fc in range(4):
            units.append(mlp_fc_unit(fc))

        def g_unit(t):
            def u(t=t):
                xb = B_x0[t % 2]
                P.dma("sync", x0_t[:, (t % 2) * D:(t % 2 + 1) * D],
                      x_in.ap()[t0 + t * 128: t0 + (t + 1) * 128, :],
                      writes=[xb])
                ln_tile(out1_t[:, t * D:(t + 1) * D], b_out1, D,
                        y_blk[:, 0:D], b_yblk, lane=0)
                V(lambda e, t=t: e.tensor_tensor(
                    x0_t[:, (t % 2) * D:(t % 2 + 1) * D], y_blk[:, 0:D],
                    x0_t[:, (t % 2) * D:(t % 2 + 1) * D], ALU.add),
                  reads=[b_yblk, xb], writes=[xb])
                P.dma("sync", x1_d.ap()[t0 + t * 128: t0 + (t + 1) * 128, :],
                      x0_t[:, (t % 2) * D:(t % 2 + 1) * D], reads=[xb],
                      writes=[b_x1d])
                if stage == 1:
                    P.dma("sync",
                          y_out.ap()[t0 + t * 128: t0 + (t + 1) * 128, :],
                          x0_t[:, (t % 2) * D:(t % 2 + 1) * D], reads=[xb])
            return u

        for t in range(NB):
            units.append(g_unit(t))
        return units

    def interleave(pa, pb_):
        na, nb = len(pa), len(pb_)
        ib = 0
        for ia, uu in enumerate(pa):
            uu()
            want = int(nb * (ia + 1) / na)
            while ib < want:
                pb_[ib]()
                ib += 1
        while ib < nb:
            pb_[ib]()
            ib += 1

    for uu in prep_units(0):
        uu()
    for ch in range(NCH):
        z_part(ch)
        pend = None
        for b in range(NB):
            pend = scan_block(ch, b, pend)
        pend[0]()
        for cg in pend[1]:
            cg()
        pu = prep_units(ch + 1) if ch + 1 < NCH else []
        interleave(post_units(ch), pu)

    if stage >= 2:
        P.fence()
        bcast_from_dram(bc_a, b_bc_a, w_l1g, 2)
        for i in range(2):
            P.dma("sync", lnrow_s[:],
                  w_l1b.ap().rearrange("(o d) -> o d", o=1)[0:1,
                                                           i * 512:(i + 1) * 512],
                  reads=[b_lnrow], writes=[b_lnrow])
            bcast_row(bc_a[:, D + i * 512:D + (i + 1) * 512], b_bc_a, lnrow_s[:],
                      b_lnrow, 512)
        V(lambda e: e.memset(
            vsb_t[:].rearrange("p (a c) -> p a c", c=65)[:, :, 64:65], 1.0),
          writes=[b_vsb])
        for ch in range(NCH):
            t0 = ch * CH
            for t in range(NB):
                xb = B_x0[t % 2]
                P.dma("sync", x0_t[:, (t % 2) * D:(t % 2 + 1) * D],
                      x1_d.ap()[t0 + t * 128: t0 + (t + 1) * 128, :],
                      reads=[b_x1d], writes=[xb])
                src_ap = x0_t[:, (t % 2) * D:(t % 2 + 1) * D]
                ln_tile(src_ap, xb, D, scr_t[:, 0:D], b_scr)
                ln_tile(scr_t[:, 0:D], b_scr, D, scr_t[:, 0:D], b_scr,
                        affine=(bc_a[:, 0:D], bc_a[:, D:2 * D], b_bc_a))
                cast_transpose(scr_t[:, 0:D], b_scr, 8, xnfm_t, b_xnfm,
                               t * 128, CH)
            for part, dst, bdst in ((1, kfm_t, b_kfm), (0, None, None)):
                for ftl in range(8):
                    c0 = part * D + ftl * 128
                    wb = B_ipw[ftl % 2]
                    srcw = w_qkv.ap()[:, c0:c0 + 128].rearrange(
                        "(dc p) f -> p dc f", p=128)
                    P.dma("gpsimd",
                          ipw_t[ftl % 2][:].rearrange("p (dc f) -> p dc f", f=128),
                          srcw, writes=[wb])
                    bank = B_pb[ftl % 2]
                    for d in range(8):
                        TE(lambda e, ftl=ftl, d=d: e.matmul(
                            pb[ftl % 2][:, 0:CH],
                            ipw_t[ftl % 2][:, d * 128:(d + 1) * 128],
                            xnfm_t[:, d * CH:(d + 1) * CH],
                            start=(d == 0), stop=(d == 7)),
                           reads=[wb, b_xnfm], writes=[bank])
                    if part == 1:
                        V(lambda e, ftl=ftl, ch=ch: e.tensor_copy(
                            kfm_t[:, ftl * T + ch * CH: ftl * T + (ch + 1) * CH],
                            pb[ftl % 2][:, 0:CH]),
                          reads=[bank], writes=[b_kfm])
                    else:
                        V(lambda e, ftl=ftl: e.tensor_copy(
                            scrb_t[:, 0:CH], pb[ftl % 2][:, 0:CH]),
                          reads=[bank], writes=[b_scrb])
                        P.dma("sync",
                              q_d.ap()[ftl, :, ch * CH:(ch + 1) * CH],
                              scrb_t[:, 0:CH], reads=[b_scrb], writes=[b_qd])
            for half in range(2):
                for dh in range(2):
                    src_vw = w_qkv.ap()[dh * 512:(dh + 1) * 512,
                                        2 * D + half * 512:
                                        2 * D + (half + 1) * 512].rearrange(
                        "(dc p) f -> p dc f", p=128)
                    P.dma("gpsimd",
                          vw_t[:].rearrange("p (dc f) -> p dc f", f=512),
                          src_vw, writes=[b_vw])
                    for t in range(NB):
                        for d in range(4):
                            TE(lambda e, d=d, dh=dh, t=t: e.matmul(
                                pb[t][:, 0:512],
                                xnfm_t[:, (dh * 4 + d) * CH + t * 128:
                                       (dh * 4 + d) * CH + (t + 1) * 128],
                                vw_t[:, d * 512:(d + 1) * 512],
                                start=(dh == 0 and d == 0),
                                stop=(dh == 1 and d == 3)),
                               reads=[b_vw, b_xnfm], writes=[B_pb[t]])
                for t in range(NB):
                    kt = ch * NB + t
                    dst = vsb_t[:, kt * 1040 + half * 8 * 65:
                                kt * 1040 + (half + 1) * 8 * 65] \
                        .rearrange("p (a c) -> p a c", c=65)[:, :, 0:64]
                    srcp = pb[t][:, 0:512].rearrange("p (a c) -> p a c", c=64)
                    V(lambda e, dst=dst, srcp=srcp, t=t: e.tensor_copy(
                        dst, srcp),
                      reads=[B_pb[t]], writes=[b_vsb])

    if stage >= 2:
        P.fence()
        negc_s, b_negc = sb("negc_s", [128, 1], F32)
        V(lambda e: e.memset(negc_s[:], -ATT_C), writes=[b_negc])
        P.dma("sync", mask4_s[:], amask_c.ap(), reads=[b_mask4],
              writes=[b_mask4])
        for qg in range(8):
            nk = 4 * qg + 4
            pend_ep = []
            for p in range(8):
                qb = B_qsb[p % 2]
                P.dma("sync", qsb_t[p % 2][:],
                      q_d.ap()[p, :, qg * 512:(qg + 1) * 512],
                      reads=[b_qd], writes=[qb])
                AB = (4, 5) if p % 2 == 0 else (2, 3)

                def emit_scores(hi, kb, p=p, qb=qb):
                    h = 2 * p + hi
                    hh = hi * 64
                    sb_i = hi
                    pt_i = 2 * hi + kb % 2
                    diag = kb >= 4 * qg
                    off = (kb - 4 * qg) * 128 if diag else 0
                    wdt = 512 - off
                    sbank = B_pb[sb_i]
                    TE(lambda e, h=h, hh=hh, kb=kb, off=off, wdt=wdt,
                       sb_i=sb_i: e.matmul(
                           pb[sb_i][:, 0:wdt],
                           kfm_t[hh:hh + 64, p * T + kb * 128:
                                 p * T + (kb + 1) * 128],
                           qsb_t[p % 2][hh:hh + 64, off:512],
                           start=True, stop=True),
                       reads=[b_kfm, qb], writes=[sbank])
                    if diag:
                        V(lambda e, sb_i=sb_i, wdt=wdt: e.tensor_tensor(
                            rc_t[:, 0:wdt],
                            pb[sb_i][:, 0:wdt], mask4_s[:, 0:wdt], ALU.add),
                          reads=[sbank, b_mask4], writes=[b_rc])
                        S(lambda e, pt_i=pt_i, off=off, wdt=wdt: e.activation(
                            pt_t[pt_i][:, off:512], rc_t[:, 0:wdt], AF.Exp,
                            bias=negc_s[:], scale=0.125),
                          reads=[b_rc, b_negc], writes=[B_pt[pt_i]])
                    else:
                        S(lambda e, pt_i=pt_i, sb_i=sb_i, wdt=wdt:
                          e.activation(
                              pt_t[pt_i][:, 0:512], pb[sb_i][:, 0:wdt],
                              AF.Exp, bias=negc_s[:], scale=0.125),
                          reads=[sbank, b_negc], writes=[B_pt[pt_i]])

                def emit_pv(hi, kb, p=p, AB=AB):
                    h = 2 * p + hi
                    ob = AB[hi]
                    pt_i = 2 * hi + kb % 2
                    diag = kb >= 4 * qg
                    off = (kb - 4 * qg) * 128 if diag else 0
                    TE(lambda e, h=h, kb=kb, off=off, ob=ob, pt_i=pt_i:
                       e.matmul(
                           pb[ob][0:65, off:512],
                           vsb_t[:, kb * 1040 + h * 65:
                                 kb * 1040 + (h + 1) * 65],
                           pt_t[pt_i][:, off:512],
                           start=(kb == 0), stop=(kb == nk - 1)),
                       reads=[b_vsb, B_pt[pt_i]], writes=[B_pb[ob]])

                emit_scores(0, 0)
                emit_scores(1, 0)
                for ep in pend_ep:
                    ep()
                pend_ep = []
                for kb in range(1, nk):
                    emit_scores(0, kb)
                    emit_pv(0, kb - 1)
                    emit_scores(1, kb)
                    emit_pv(1, kb - 1)
                emit_pv(0, nk - 1)
                emit_pv(1, nk - 1)

                def mk_ep(hi, p=p, AB=AB):
                    def ep():
                        ob = AB[hi]
                        hh = hi * 64
                        V(lambda e, ob=ob: e.reciprocal(
                            rl_s[:], pb[ob][64:65, 0:512]),
                          reads=[B_pb[ob]], writes=[b_rl])
                        TE(lambda e: e.matmul(ptr2[:, 0:512],
                                              ones_fs[0:1, 0:128],
                                              rl_s[:], start=True, stop=True),
                           reads=[b_ones_f, b_rl], writes=[b_ptr2])
                        S(lambda e: e.activation(rc_t, ptr2[:, 0:512],
                                                 AF.Copy),
                          reads=[b_ptr2], writes=[b_rc])
                        V(lambda e, hh=hh, ob=ob, p=p: e.tensor_tensor(
                            hr_t[hh:hh + 64, p * 512:(p + 1) * 512],
                            pb[ob][0:64, 0:512], rc_t[0:64, 0:512], ALU.mult),
                          reads=[B_pb[ob], b_rc], writes=[b_hr])
                    return ep

                pend_ep = [mk_ep(0), mk_ep(1)]
            for ep in pend_ep:
                ep()
            pend_ep = []
            for tp in range(2):
                for half in range(2):
                    for k in range(8):
                        wb = B_wst[k % 2]
                        P.dma("gpsimd", wst_t[k % 2][:],
                              w_wo.ap()[k * 128:(k + 1) * 128,
                                        half * 512:(half + 1) * 512],
                              writes=[wb])
                        for ti in range(2):
                            t = 2 * tp + ti
                            TE(lambda e, k=k, t=t, ti=ti, half=half: e.matmul(
                                pb[ti * 2 + half][:, 0:512],
                                hr_t[:, k * 512 + t * 128:
                                     k * 512 + (t + 1) * 128],
                                wst_t[k % 2][:],
                                start=(k == 0), stop=(k == 7)),
                               reads=[wb, b_hr], writes=[B_pb[ti * 2 + half]])
                for ti in range(2):
                    t = 2 * tp + ti
                    xb = B_x0[t % 2]
                    tg = qg * 512 + t * 128
                    P.dma("sync", x0_t[:, (t % 2) * D:(t % 2 + 1) * D],
                          x1_d.ap()[tg:tg + 128, :], reads=[b_x1d], writes=[xb])
                    ln_tile(x0_t[:, (t % 2) * D:(t % 2 + 1) * D], xb, D,
                            scr_t[:, 0:D], b_scr)
                    for half in range(2):
                        V(lambda e, ti=ti, half=half: e.tensor_tensor(
                            scr_t[:, half * 512:(half + 1) * 512],
                            pb[ti * 2 + half][:, 0:512],
                            scr_t[:, half * 512:(half + 1) * 512], ALU.add),
                          reads=[B_pb[ti * 2 + half], b_scr], writes=[b_scr])
                    P.dma("sync", xres_d.ap()[tg:tg + 128, :], scr_t[:, 0:D],
                          reads=[b_scr], writes=[b_xresd])
                    if stage == 2:
                        P.dma("sync", y_out.ap()[tg:tg + 128, :], scr_t[:, 0:D],
                              reads=[b_scr])

    if stage >= 3:
        P.fence()
        bcast_from_dram(bc_a, b_bc_a, w_l2g, 2)
        for i in range(2):
            P.dma("sync", lnrow_s[:],
                  w_l2b.ap().rearrange("(o d) -> o d", o=1)[0:1,
                                                           i * 512:(i + 1) * 512],
                  reads=[b_lnrow], writes=[b_lnrow])
            bcast_row(bc_a[:, D + i * 512:D + (i + 1) * 512], b_bc_a, lnrow_s[:],
                      b_lnrow, 512)
        P.dma("gpsimd",
              gw_s[:].rearrange("p (dc f) -> p dc f", f=8),
              w_gate.ap().rearrange("(dc p) f -> p dc f", p=128), writes=[b_gw])
        def yg_view(t):
            return gates_t[:, t * NE:(t + 1) * NE]

        for half in range(2):
            t0 = half * 2048
            if half == 0:
                for ex in range(NE):
                    P.dma("sync", lfm_s[:],
                          w_eb1.ap()[ex, :].rearrange("(c p) -> c p", p=128),
                          reads=[b_lfm], writes=[b_lfm])
                    TE(lambda e: e.transpose(ptr2[:, 0:32], lfm_s[:],
                                             id_f[0:32, 0:32]),
                       reads=[b_lfm, b_id_f], writes=[b_ptr2])
                    V(lambda e, ex=ex: e.tensor_copy(
                        mb1a_t[:, ex * 32:(ex + 1) * 32], ptr2[:, 0:32]),
                      reads=[b_ptr2], writes=[b_mb1a])

            def emit_wdma(s):
                ex, fc = divmod(s, 8)
                par = s % 2
                c0 = fc * 512
                for kh in range(2):
                    P.dma("gpsimd",
                          ew2_t[par][:, kh * 2 * D:(kh + 1) * 2 * D].rearrange(
                              "p (kc d) -> p kc d", d=D),
                          w_e2.ap()[ex, c0 + kh * 256:c0 + (kh + 1) * 256,
                                    :].rearrange("(kc p) d -> p kc d", p=128),
                          writes=[B_ew2[par][kh]])
                for f in range(4):
                    srcw = w_e1.ap()[ex, :,
                                     c0 + f * 128:c0 + (f + 1) * 128] \
                        .rearrange("(dc p) f -> p dc f", p=128)
                    P.dma("gpsimd",
                          ew1_t[par][:, f * 1024:(f + 1) * 1024].rearrange(
                              "p (dc c) -> p dc c", c=128),
                          srcw, writes=[B_ew1[par][f]])
                if fc == 0:
                    P.dma("gpsimd", mb2a_t[ex % 2][:],
                          w_eb2.ap()[ex, :].rearrange("(o d) -> o d", o=1),
                          writes=[B_mb2a[ex % 2]])

            emit_wdma(0)
            for t in range(16):
                lane = t % 2
                st = st_t[:, lane * 8:lane * 8 + 8]
                sc = st_t[:, 16 + lane * 16:32 + lane * 16]
                bst = ST[lane]
                bi = 4 + lane
                xb = B_x0[t % 2]
                P.dma("sync", x0_t[:, (t % 2) * D:(t % 2 + 1) * D],
                      xres_d.ap()[t0 + t * 128: t0 + (t + 1) * 128, :],
                      reads=[b_xresd], writes=[xb])
                V(lambda e, t=t: e.tensor_copy(
                    acc_t[:, t * D:(t + 1) * D],
                    x0_t[:, (t % 2) * D:(t % 2 + 1) * D]),
                  reads=[xb], writes=[b_acc])
                ln_tile(x0_t[:, (t % 2) * D:(t % 2 + 1) * D], xb, D,
                        scr_t[:, 0:D], b_scr,
                        affine=(bc_a[:, 0:D], bc_a[:, D:2 * D], b_bc_a),
                        lane=lane,
                        out_bf=(scrb_t[:, lane * 1024:(lane + 1) * 1024],
                                SCRB[lane]))
                cast_transpose(scrb_t[:, lane * 1024:(lane + 1) * 1024],
                               SCRB[lane], 8, mfm_t, b_mfm,
                               t * 128, 2048, lane=lane, src_is_bf=True)
                for d in range(8):
                    TE(lambda e, t=t, d=d, bi=bi: e.matmul(
                        pb[bi][:, 0:NE],
                        mfm_t[:, d * 2048 + t * 128: d * 2048 + (t + 1) * 128],
                        gw_s[:, d * 8:(d + 1) * 8],
                        start=(d == 0), stop=(d == 7)),
                       reads=[b_mfm, b_gw], writes=[B_pb[bi]])
                V(lambda e, bi=bi, st=st: e.tensor_reduce(
                    st[:, 0:1], pb[bi][:, 0:NE], AX, ALU.max),
                  reads=[B_pb[bi]], writes=[bst])
                V(lambda e, st=st: e.tensor_scalar(st[:, 0:1], st[:, 0:1],
                                                   -1.0, None, ALU.mult),
                  reads=[bst], writes=[bst])
                S(lambda e, t=t, bi=bi, st=st: e.activation(
                    yg_view(t), pb[bi][:, 0:NE], AF.Exp, bias=st[:, 0:1]),
                  reads=[B_pb[bi], bst], writes=[b_gates])
                V(lambda e, t=t, st=st: e.tensor_reduce(
                    st[:, 1:2], yg_view(t), AX, ALU.max),
                  reads=[b_gates], writes=[bst])
                V(lambda e, st=st: e.tensor_scalar(st[:, 2:3], st[:, 1:2],
                                                   0.999999, None, ALU.mult),
                  reads=[bst], writes=[bst])
                V(lambda e, t=t, st=st, sc=sc: e.tensor_scalar(
                    sc[:, 0:NE], yg_view(t), st[:, 2:3], None, ALU.is_ge),
                  reads=[b_gates, bst], writes=[bst])
                V(lambda e, t=t, sc=sc: e.scalar_tensor_tensor(
                    sc[:, 8:16], sc[:, 0:NE], -1e30, yg_view(t),
                    ALU.mult, ALU.add),
                  reads=[bst, b_gates], writes=[bst])
                V(lambda e, st=st, sc=sc: e.tensor_reduce(
                    st[:, 3:4], sc[:, 8:16], AX, ALU.max),
                  reads=[bst], writes=[bst])
                V(lambda e, st=st: e.tensor_scalar(st[:, 4:5], st[:, 3:4],
                                                   0.999999, None, ALU.mult),
                  reads=[bst], writes=[bst])
                V(lambda e, t=t, st=st, sc=sc: e.tensor_scalar(
                    sc[:, 0:NE], yg_view(t), st[:, 4:5], None, ALU.is_ge),
                  reads=[b_gates, bst], writes=[bst])
                V(lambda e, st=st: e.tensor_tensor(st[:, 5:6], st[:, 1:2],
                                                   st[:, 3:4], ALU.add),
                  reads=[bst], writes=[bst])
                V(lambda e, st=st: e.reciprocal(st[:, 5:6], st[:, 5:6]),
                  reads=[bst], writes=[bst])
                V(lambda e, t=t, sc=sc: e.tensor_tensor(
                    yg_view(t), yg_view(t), sc[:, 0:NE], ALU.mult),
                  reads=[bst, b_gates], writes=[b_gates])
                V(lambda e, t=t, st=st: e.tensor_scalar(
                    yg_view(t), yg_view(t), st[:, 5:6], None, ALU.mult),
                  reads=[bst, b_gates], writes=[b_gates])
            NS = NE * 8
            ci = [0]
            for s in range(NS):
                ex, fc = divmod(s, 8)
                par = s % 2
                if s + 1 < NS:
                    emit_wdma(s + 1)
                last = (fc == 7)

                def w1_grp(tg, ex=ex, fc=fc, par=par):
                    pair = (4, 5) if tg % 2 == 0 else (2, 3)
                    hrbase = (tg % 2) * 2048
                    bhr = B_hrp[tg % 2]
                    for f in range(4):
                        bi = pair[f % 2]
                        bank = B_pb[bi]
                        for d in range(8):
                            TE(lambda e, f=f, d=d, tg=tg, bi=bi, par=par:
                               e.matmul(
                                   pb[bi][:, 0:512],
                                   ew1_t[par][:, (f * 8 + d) * 128:
                                              (f * 8 + d + 1) * 128],
                                   mfm_t[:, d * 2048 + tg * 512:
                                         d * 2048 + (tg + 1) * 512],
                                   start=(d == 0), stop=(d == 7)),
                               reads=[B_ew1[par][f], b_mfm], writes=[bank])
                        idx = fc * 4 + f
                        S(lambda e, f=f, idx=idx, ex=ex, bi=bi, hrbase=hrbase:
                          e.activation(
                              hr_t[:, hrbase + f * 512:hrbase + (f + 1) * 512],
                              pb[bi][:, 0:512],
                              AF.Gelu_apprx_tanh,
                              bias=mb1a_t[:, ex * 32 + idx:ex * 32 + idx + 1]),
                          reads=[bank, b_mb1a], writes=[bhr])

                def w2_grp(tg, ex=ex, par=par, last=last):
                    hrbase = (tg % 2) * 2048
                    bhr = B_hrp[tg % 2]
                    for t in range(NB):
                        tt = tg * 4 + t
                        for hf in range(2):
                            cb = ci[0] % 2
                            ci[0] += 1
                            for k in range(4):
                                TE(lambda e, k=k, t=t, hf=hf, cb=cb, par=par,
                                   hrbase=hrbase: e.matmul(
                                       pb[cb][:, 0:512],
                                       hr_t[:, hrbase + k * 512 + t * 128:
                                            hrbase + k * 512 + (t + 1) * 128],
                                       ew2_t[par][:, k * D + hf * 512:
                                                  k * D + (hf + 1) * 512],
                                       start=(k == 0),
                                       stop=(k == 3 and not last)),
                                   reads=[B_ew2[par][k // 2], bhr],
                                   writes=[B_pb[cb]])
                            if last:
                                TE(lambda e, cb=cb, hf=hf, ex=ex: e.matmul(
                                    pb[cb][:, 0:512], ones_bs[0:1, 0:128],
                                    mb2a_t[ex % 2][0:1,
                                                   hf * 512:(hf + 1) * 512],
                                    start=False, stop=True),
                                   reads=[b_ones_b, B_mb2a[ex % 2]],
                                   writes=[B_pb[cb]])
                            V(lambda e, cb=cb, tt=tt, hf=hf, ex=ex:
                              e.scalar_tensor_tensor(
                                  acc_t[:, tt * D + hf * 512:
                                        tt * D + (hf + 1) * 512],
                                  pb[cb][:, 0:512],
                                  gates_t[:, tt * NE + ex:tt * NE + ex + 1],
                                  acc_t[:, tt * D + hf * 512:
                                        tt * D + (hf + 1) * 512],
                                  ALU.mult, ALU.add),
                              reads=[B_pb[cb], b_gates, b_acc],
                              writes=[b_acc])

                w1_grp(0)
                w1_grp(1)
                w2_grp(0)
                w1_grp(2)
                w2_grp(1)
                w1_grp(3)
                w2_grp(2)
                w2_grp(3)
            for t in range(16):
                xb = B_x0[t % 2]
                P.dma("sync", x0_t[:, (t % 2) * D:(t % 2 + 1) * D],
                      x1_d.ap()[t0 + t * 128: t0 + (t + 1) * 128, :],
                      reads=[b_x1d], writes=[xb])
                ln_tile(acc_t[:, t * D:(t + 1) * D], b_acc, D, scr_t[:, 0:D],
                        b_scr)
                V(lambda e, t=t: e.tensor_tensor(
                    scr_t[:, 0:D], scr_t[:, 0:D],
                    x0_t[:, (t % 2) * D:(t % 2 + 1) * D], ALU.add),
                  reads=[xb, b_scr], writes=[b_scr])
                P.dma("sync", y_out.ap()[t0 + t * 128: t0 + (t + 1) * 128, :],
                      scr_t[:, 0:D], reads=[b_scr])

    P.finish()
    P.emit()
    return nc


_NC_CACHE = {}


def _get_nc(stage):
    if stage not in _NC_CACHE:
        nc = bass.Bass("TRN2", target_bir_lowering=False)
        build(nc, stage)
        _NC_CACHE[stage] = nc
    return _NC_CACHE[stage]


def kernel(**inputs):
    nc = _get_nc(STAGE)
    x = np.asarray(inputs["L0_attn_x"])
    names = {
        "attn_ln_w": "L0_attn_ln_w", "attn_ln_b": "L0_attn_ln_b",
        "attn_in_proj_w": "L0_attn_in_proj_w", "attn_conv_w": "L0_attn_conv_w",
        "attn_conv_b": "L0_attn_conv_b", "attn_dt_bias": "L0_attn_dt_bias",
        "attn_A_log": "L0_attn_A_log", "attn_D_skip": "L0_attn_D_skip",
        "attn_rms_w": "L0_attn_rms_w", "attn_out_proj_w": "L0_attn_out_proj_w",
        "attn_mlp_w1": "L0_attn_mlp_w1", "attn_mlp_b1": "L0_attn_mlp_b1",
        "attn_mlp_w2": "L0_attn_mlp_w2", "attn_mlp_b2": "L0_attn_mlp_b2",
        "ffn_ln1_g": "L0_ffn_ln1_g", "ffn_ln1_b": "L0_ffn_ln1_b",
        "ffn_ln2_g": "L0_ffn_ln2_g", "ffn_ln2_b": "L0_ffn_ln2_b",
        "ffn_w_qkv": "L0_ffn_w_qkv", "ffn_w_o": "L0_ffn_w_o",
        "ffn_w_gate": "L0_ffn_w_gate", "ffn_w1": "L0_ffn_w1",
        "ffn_b1": "L0_ffn_b1", "ffn_w2": "L0_ffn_w2", "ffn_b2": "L0_ffn_b2",
    }
    shared = {k: np.ascontiguousarray(np.asarray(inputs[v], dtype=np.float32))
              for k, v in names.items()}
    in_maps = []
    for b in range(8):
        m = dict(shared)
        m["x"] = np.ascontiguousarray(x[b].astype(np.float32))
        in_maps.append(m)
    res = run_bass_kernel_spmd(nc, in_maps, core_ids=list(range(8)))
    return np.stack([r["y"] for r in res.results], axis=0)


# version 16
# speedup vs baseline: 1.0455x; 1.0455x over previous
import numpy as np
import ml_dtypes
import concourse.bass as bass
import concourse.mybir as mybir
from concourse.bass_utils import run_bass_kernel_spmd

F32 = mybir.dt.float32
BF16 = mybir.dt.bfloat16
AF = mybir.ActivationFunctionType
ALU = mybir.AluOpType
AX = mybir.AxisListType.X

T = 4096
D = 1024
CH = 512
NCH = T // CH
NB = CH // 128
H = 32
HD = 64
DS = 128
DIN = 2048
CONVD = 2304
DIP = 4384
DFF = 4096
AH = 16
NE = 8
NEG = -1.0e38
ATT_C = 30.0

ENGS = ("tensor", "vector", "scalar", "gpsimd", "sync")
STAGE = 3


class Buf:
    __slots__ = ("w", "rs", "name")

    def __init__(self, name=""):
        self.w = None
        self.rs = []
        self.name = name


class Prog:
    def __init__(self, nc):
        self.nc = nc
        self.cl = {e: [] for e in ENGS}
        self.cnt = {e: 0 for e in ENGS}
        self.sem = {e: nc.alloc_semaphore(f"sem_{e}") for e in ENGS}
        self.dsem = {e: nc.alloc_semaphore(f"dma_{e}")
                     for e in ("sync", "gpsimd", "scalar")}
        self.dcnt = {e: 0 for e in ("sync", "gpsimd", "scalar")}
        self.wm = {e: {} for e in ENGS}

    SELF_SYNC = ("vector", "scalar", "gpsimd")

    def _waits(self, e, deps):
        out = {}
        own = self.sem[e]
        skip_own = e not in self.SELF_SYNC
        for d in deps:
            if d is None:
                continue
            s, v = d
            if s is own and skip_own:
                continue
            if out.get(s, 0) < v:
                out[s] = v
        wm = self.wm[e]
        res = []
        for s, v in out.items():
            if wm.get(s, 0) >= v:
                continue
            wm[s] = v
            res.append((s, v))
        return res

    def op(self, e, fn, reads=(), writes=(), extra=()):
        deps = [b.w for b in reads]
        for b in writes:
            deps += b.rs
            deps.append(b.w)
        deps += list(extra)
        waits = self._waits(e, deps)
        self.cnt[e] += 1
        tok = (self.sem[e], self.cnt[e])
        sem = self.sem[e]

        def emit(eng, waits=waits, fn=fn, sem=sem):
            for s, v in waits:
                eng.wait_ge(s, v)
            fn(eng).then_inc(sem, 1)

        self.cl[e].append(emit)
        for b in reads:
            b.rs.append(tok)
        for b in writes:
            b.w = tok
            b.rs = []
        return tok

    def dma(self, q, out_ap, in_ap, reads=(), writes=(), extra=()):
        deps = [b.w for b in reads]
        for b in writes:
            deps += b.rs
            deps.append(b.w)
        deps += list(extra)
        waits = self._waits(q, deps)
        dsem = self.dsem[q]
        self.dcnt[q] += 1
        tok = (dsem, 16 * self.dcnt[q])

        def emit(eng, waits=waits, o=out_ap, i=in_ap, dsem=dsem):
            for s, v in waits:
                eng.wait_ge(s, v)
            eng.dma_start(o, i).then_inc(dsem, 16)

        self.cl[q].append(emit)
        for b in reads:
            b.rs.append(tok)
        for b in writes:
            b.w = tok
            b.rs = []
        return tok

    def fence(self):
        toks = [(self.sem[e], self.cnt[e]) for e in ENGS]
        toks += [(self.dsem[q], 16 * self.dcnt[q]) for q in
                 ("sync", "gpsimd", "scalar")]
        for e in ENGS:
            waits = self._waits(e, toks)
            if not waits:
                continue

            def emit(eng, waits=waits):
                for s, v in waits:
                    eng.wait_ge(s, v)

            self.cl[e].append(emit)

    def finish(self):
        for q in ("sync", "gpsimd", "scalar"):
            dsem = self.dsem[q]
            v = 16 * self.dcnt[q]
            if v == 0:
                continue

            def emit(eng, dsem=dsem, v=v):
                eng.wait_ge(dsem, v)

            self.cl["sync"].append(emit)

    def emit(self):
        nc = self.nc
        with nc.Block() as block:
            @block.tensor
            def _(t):
                for f in self.cl["tensor"]:
                    f(t)

            @block.vector
            def _(v):
                for f in self.cl["vector"]:
                    f(v)

            @block.scalar
            def _(s):
                for f in self.cl["scalar"]:
                    f(s)

            @block.gpsimd
            def _(g):
                for f in self.cl["gpsimd"]:
                    f(g)

            @block.sync
            def _(sy):
                for f in self.cl["sync"]:
                    f(sy)


class Region:

    def __init__(self, nc, off0, cap):
        self.nc = nc
        self.off0 = off0
        self.cap = cap
        self.cur = off0

    def reset(self):
        self.cur = off0 if False else self.off0

    def at(self, name, shape, dt):
        import math
        itm = {F32: 4, BF16: 2}[dt]
        nbytes = shape[1] * itm
        nbytes = (nbytes + 31) & ~31
        off = self.cur
        assert off + nbytes <= self.cap, (name, off, nbytes, self.cap)
        self.cur = off + nbytes
        return self.nc.alloc_sbuf_tensor_at(name, shape, dt, offset=off)


def build(nc, stage=STAGE, debug=()):
    P = Prog(nc)
    dbg_done = set()

    def DBG(name, ap, bufs):
        if name in debug and name not in dbg_done:
            dbg_done.add(name)
            shape = [ap.shape[i] for i in range(len(ap.shape))]
            t = nc.dram_tensor("dbg_" + name, shape, ap.dtype, kind="ExternalOutput")
            P.dma("sync", t.ap(), ap, reads=list(bufs))

    x_in = nc.dram_tensor("x", [T, D], F32, kind="ExternalInput")
    y_out = nc.dram_tensor("y", [T, D], F32, kind="ExternalOutput")

    def din(name, shape):
        return nc.dram_tensor(name, shape, F32, kind="ExternalInput")

    w_ln_w = din("attn_ln_w", [D])
    w_ln_b = din("attn_ln_b", [D])
    w_inproj = din("attn_in_proj_w", [D, DIP])
    w_conv = din("attn_conv_w", [CONVD, 4])
    w_convb = din("attn_conv_b", [CONVD])
    w_dtb = din("attn_dt_bias", [H])
    w_alog = din("attn_A_log", [H])
    w_dskip = din("attn_D_skip", [H])
    w_rms = din("attn_rms_w", [DIN])
    w_outp = din("attn_out_proj_w", [DIN, D])
    w_m1 = din("attn_mlp_w1", [D, DFF])
    w_mb1 = din("attn_mlp_b1", [DFF])
    w_m2 = din("attn_mlp_w2", [DFF, D])
    w_mb2 = din("attn_mlp_b2", [D])
    w_l1g = din("ffn_ln1_g", [D])
    w_l1b = din("ffn_ln1_b", [D])
    w_l2g = din("ffn_ln2_g", [D])
    w_l2b = din("ffn_ln2_b", [D])
    w_qkv = din("ffn_w_qkv", [D, 3 * D])
    w_wo = din("ffn_w_o", [D, D])
    w_gate = din("ffn_w_gate", [D, NE])
    w_e1 = din("ffn_w1", [NE, D, DFF])
    w_eb1 = din("ffn_b1", [NE, DFF])
    w_e2 = din("ffn_w2", [NE, DFF, D])
    w_eb2 = din("ffn_b2", [NE, D])

    x1_d = nc.dram_tensor("x1_scr", [T, D], F32, kind="Internal")
    q_d = nc.dram_tensor("q_scr", [8, 128, T], BF16, kind="Internal")
    xres_d = nc.dram_tensor("xres_scr", [T, D], F32, kind="Internal")
    u1_d = nc.dram_tensor("u1_scr", [T, D], F32, kind="Internal")
    b_x1d = Buf("x1_d")
    b_qd = Buf("q_d")
    b_xresd = Buf("xres_d")
    b_u1d = Buf("u1_d")

    lt_np = np.tril(np.ones((128, 128), dtype=np.float32)).T
    ident_bf = nc.inline_tensor(np.eye(128, dtype=ml_dtypes.bfloat16), "c_ident_bf")
    ident_f = nc.inline_tensor(np.eye(128, dtype=np.float32), "c_ident_f")
    lt_c = nc.inline_tensor(np.ascontiguousarray(lt_np), "c_lt")
    m1 = np.where(lt_np > 0, 0.0, NEG).astype(np.float32)
    mask4_c = nc.inline_tensor(np.ascontiguousarray(np.tile(m1, (1, 4))), "c_mask4")
    am = np.zeros((128, 512), dtype=np.float32)
    am[:, 0:128] = m1
    amask_c = nc.inline_tensor(np.ascontiguousarray(am), "c_amask")
    ones_f_c = nc.inline_tensor(np.ones((1, 512), dtype=np.float32), "c_ones_f")
    ones_b_c = nc.inline_tensor(np.ones((1, 512), dtype=ml_dtypes.bfloat16), "c_ones_b")

    def sb(name, shape, dt=F32):
        return nc.alloc_sbuf_tensor(name, shape, dt), Buf(name)

    id_bf, b_id_bf = sb("id_bf", [128, 128], BF16)
    id_f, b_id_f = sb("id_f", [128, 128], F32)
    lt_s, b_lt = sb("lt_s", [128, 128], F32)
    mask4_s, b_mask4 = sb("mask4_s", [128, 512], F32)
    ones_fs, b_ones_f = sb("ones_fs", [1, 512], F32)
    ones_bs, b_ones_b = sb("ones_bs", [1, 512], BF16)

    bc_a, b_bc_a = sb("bc_a", [128, 2048], F32)
    bc_rms, b_bc_rms = sb("bc_rms", [128, 2048], F32)

    dtb_s, b_dtb = sb("dtb_s", [H, 1], F32)
    alog_s, b_alog = sb("alog_s", [H, 1], F32)
    a_s, b_a = sb("a_s", [H, 1], F32)
    drow_s, b_drow = sb("drow_s", [1, H], F32)
    dcol_s, b_dcol = sb("dcol_s", [128, H], F32)
    convw_s, b_convw = sb("convw_s", [128, 18 * 4], F32)
    convb_s, b_convb = sb("convb_s", [128, 18], F32)
    lnrow_s, b_lnrow = sb("lnrow_s", [1, 512], F32)

    pb, B_pb = [], []
    for i in range(6):
        t = nc.alloc_psum_tensor(f"pb{i}", [128, 512], F32)
        pb.append(t)
        B_pb.append(Buf(f"pb{i}"))
    ptr = nc.alloc_psum_tensor("ptr", [128, 1024], BF16)
    b_ptr = Buf("ptr")
    b_ptrB = Buf("ptrB")
    PTRB = [b_ptr, b_ptrB]
    ptr2 = nc.alloc_psum_tensor("ptr2", [128, 512], F32)
    b_ptr2 = Buf("ptr2")

    x0_t, _ = sb("x0_t", [128, 2 * D], F32)
    B_x0 = [Buf("x0_0"), Buf("x0_1")]
    scr_t, b_scr = sb("scr_t", [128, D], F32)
    sp_view = scr_t[0:H, 0:CH]
    b_sp = b_scr
    scrb_t, b_scrb = sb("scrb_t", [128, DIN], BF16)
    b_scrbB = Buf("scrbB")
    SCRB = [b_scrb, b_scrbB]
    mb1_t, b_mb1 = sb("mb1_t", [128, 32], F32)
    mb2_t, b_mb2 = sb("mb2_t", [1, D], BF16)
    gw_s, b_gw = sb("gw_s", [128, 64], BF16)
    st_t, b_st = sb("st_t", [128, 48], F32)
    b_stB = Buf("stB")
    ST = [b_st, b_stB]
    rms_r, b_rmsr = sb("rms_r", [128, 4], F32)
    llrow_s, b_llrow = sb("llrow_s", [1, H], F32)
    llcol_s, b_llcol = sb("llcol_s", [H, 1], F32)
    tails_s, b_tails = sb("tails_s", [128, 18 * 3], F32)
    lfm_s, b_lfm = sb("lfm_s", [H, 128], F32)
    cfm_s, b_cfm2 = sb("cfm_s", [H, 128], F32)
    ctm_s, b_ctm = sb("ctm_s", [128, H], F32)
    expl_t, b_expl = sb("expl_t", [128, H], F32)
    ehat_t, b_ehat = sb("ehat_t", [128, H], F32)
    elast_t, b_elast = sb("elast_t", [128, H], F32)
    bmt_t, b_bmt = sb("bmt_t", [128, 128], BF16)
    b_bht = Buf("bht4")
    rl_s, b_rl = sb("rl_s", [1, 512], F32)

    off0 = nc.sbuf_base
    CAP = 229340
    reg = Region(nc, off0, CAP)
    xnfm_t = reg.at("xnfm_t", [128, 8 * CH], BF16)
    b_xnfm = Buf("xnfm")
    ipw_t = [reg.at(f"ipw{i}", [128, 8 * 128], BF16) for i in range(2)]
    B_ipw = [Buf("ipw0"), Buf("ipw1")]
    wst_t = [reg.at(f"wst{i}", [128, 512], BF16) for i in range(2)]
    B_wst = [Buf("wst0"), Buf("wst1")]
    hr_t = reg.at("hr_t", [128, 8 * CH], BF16)
    b_hr = Buf("hr")
    ovl0 = reg.cur

    sz_t = reg.at("sz_t", [128, NB * DIN], BF16)
    b_sz = Buf("sz")
    zw_t = [reg.at(f"zw{i}", [128, 4 * 512], BF16) for i in range(2)]
    B_zw = [Buf("zw0"), Buf("zw1")]
    raw_t = [reg.at(f"raw{i}", [128, 515], F32) for i in range(2)]
    B_raw = [Buf("raw0"), Buf("raw1")]
    cvo_t = reg.at("cvo_t", [128, 512], F32)
    b_cvo = Buf("cvo")
    xtm_t = reg.at("xtm_t", [128, NB * DIN], BF16)
    b_xtm = Buf("xtm")
    bmf_t = [reg.at(f"bmf_t{i}", [128, CH], BF16) for i in range(2)]
    B_bmf = [Buf("bmf0"), Buf("bmf1")]
    cmf_t = [reg.at(f"cmf_t{i}", [128, CH], BF16) for i in range(2)]
    B_cmf = [Buf("cmf0"), Buf("cmf1")]
    dt_fm = reg.at("dt_fm", [H, CH], F32)
    b_dtfm = Buf("dt_fm")
    dta_fm = reg.at("dta_fm", [H, CH], F32)
    b_dtafm = Buf("dta_fm")
    lndt_fm = [reg.at(f"lndt_fm{i}", [H, CH], F32) for i in range(2)]
    B_lndt = [Buf("lndt0"), Buf("lndt1")]
    dtt_all = [reg.at(f"dtt_all{i}", [128, NB * 2 * H], F32) for i in range(2)]
    B_dtt = [Buf("dtt0"), Buf("dtt1")]
    mlpfm_t = reg.at("mlpfm_t", [128, 8 * CH], BF16)
    b_mlpfm = Buf("mlpfm")
    u1q_t = reg.at("u1q_t", [128, D], F32)
    B_u1q = [Buf("u1q0"), Buf("u1q1")]
    bht4_t = reg.at("bht4_t", [128, 512], BF16)
    d2_t = [reg.at(f"d2_{i}", [2, 512], F32) for i in range(2)]
    B_d2 = [Buf("d2_0"), Buf("d2_1")]
    r2_t = [reg.at(f"r2_{i}", [2, 512], F32) for i in range(2)]
    B_r2 = [Buf("r2_0"), Buf("r2_1")]
    earg_t = reg.at("earg_t", [128, 512], F32)
    b_earg = Buf("earg")
    w4_t = [reg.at(f"w4_{i}", [128, 512], BF16) for i in range(2)]
    B_w4 = [Buf("w4_0"), Buf("w4_1")]
    h_st = reg.at("h_st", [128, H * HD], F32)
    b_h = Buf("h_st")
    h_bf = reg.at("h_bf", [128, H * HD], BF16)
    b_hbf = Buf("h_bf")
    yintra_t = reg.at("yintra_t", [128, 256], F32)
    b_yintra = Buf("yintra")
    y_blk = reg.at("y_blk", [128, DIN], F32)
    b_yblk = Buf("y_blk")
    ynfm_t = reg.at("ynfm_t", [128, 16 * CH], BF16)
    b_ynfm = Buf("ynfm")
    out1_t = reg.at("out1_t", [128, NB * D], F32)
    b_out1 = Buf("out1")
    p1_end = reg.cur
    opw_t = wst_t
    B_opw = B_wst
    mw2_t = wst_t
    B_mw2 = B_wst

    reg.cur = ovl0
    kfm_t = reg.at("kfm_t", [128, 8 * T], BF16)
    b_kfm = Buf("kfm")
    vsb_t = reg.at("vsb_t", [128, 32 * (16 * 65)], BF16)
    b_vsb = Buf("vsb")
    sub0 = reg.cur
    vw_t = reg.at("vw_t", [128, 4 * 512], BF16)
    b_vw = Buf("vw")
    p2_end = reg.cur
    reg.cur = sub0
    qsb_t = [reg.at(f"qsb{i}", [128, 512], BF16) for i in range(2)]
    B_qsb = [Buf("qsb0"), Buf("qsb1")]
    pt_t = [reg.at(f"pt{i}", [128, 512], BF16) for i in range(4)]
    B_pt = [Buf(f"pt{i}") for i in range(4)]
    rc_t = scr_t[:, 0:512]
    b_rc = b_scr
    p3_end = max(reg.cur, p2_end)

    reg.cur = ovl0
    acc_t = reg.at("acc_t", [128, 16 * D], F32)
    b_acc = Buf("acc")
    mfm_t = reg.at("mfm_t", [128, 8 * 2048], BF16)
    b_mfm = Buf("mfm")
    ew2_t = [reg.at(f"ew2_{i}", [128, 4 * D], BF16) for i in range(2)]
    B_ew2 = [[Buf(f"ew2_{i}_{k}") for k in range(2)] for i in range(2)]
    ew1_t = [reg.at(f"ew1_{i}", [128, 4 * 8 * 128], BF16) for i in range(2)]
    B_ew1 = [[Buf(f"ew1_{i}_{f}") for f in range(4)] for i in range(2)]
    gates_t = reg.at("gates_t", [128, 16 * NE], F32)
    b_gates = Buf("gates")
    mb1a_t = reg.at("mb1a_t", [128, 8 * 32], F32)
    b_mb1a = Buf("mb1a")
    mb2a_t = [reg.at(f"mb2a_{i}", [1, D], BF16) for i in range(2)]
    B_mb2a = [Buf("mb2a_0"), Buf("mb2a_1")]
    B_hrp = [Buf("hrp0"), Buf("hrp1")]
    p4_end = reg.cur

    def V(fn, reads=(), writes=(), extra=()):
        return P.op("vector", fn, reads, writes, extra)

    def S(fn, reads=(), writes=(), extra=()):
        return P.op("scalar", fn, reads, writes, extra)

    def TE(fn, reads=(), writes=(), extra=()):
        return P.op("tensor", fn, reads, writes, extra)

    def G(fn, reads=(), writes=(), extra=()):
        return P.op("gpsimd", fn, reads, writes, extra)

    P.dma("sync", id_bf[:], ident_bf.ap(), writes=[b_id_bf])
    P.dma("sync", id_f[:], ident_f.ap(), writes=[b_id_f])
    P.dma("sync", lt_s[:], lt_c.ap(), writes=[b_lt])
    P.dma("sync", mask4_s[:], mask4_c.ap(), writes=[b_mask4])
    P.dma("sync", ones_fs[:], ones_f_c.ap(), writes=[b_ones_f])
    P.dma("sync", ones_bs[:], ones_b_c.ap(), writes=[b_ones_b])
    P.dma("sync", dtb_s[:], w_dtb.ap().rearrange("(h o) -> h o", o=1), writes=[b_dtb])
    P.dma("sync", alog_s[:], w_alog.ap().rearrange("(h o) -> h o", o=1), writes=[b_alog])
    P.dma("sync", drow_s[:], w_dskip.ap().rearrange("(o h) -> o h", o=1), writes=[b_drow])
    P.dma("sync", convw_s[:].rearrange("p (c k) -> p c k", k=4),
          w_conv.ap().rearrange("(c p) k -> p c k", p=128), writes=[b_convw])

    S(lambda e: e.activation(a_s[:], alog_s[:], AF.Exp), reads=[b_alog], writes=[b_a])
    V(lambda e: e.tensor_scalar(a_s[:], a_s[:], -1.0, None, ALU.mult),
      reads=[b_a], writes=[b_a])

    def bcast_row(dst_ap, b_dst, src_ap, b_src, width):
        TE(lambda e: e.matmul(ptr2[:, 0:width], ones_fs[0:1, 0:128], src_ap,
                              start=True, stop=True),
           reads=[b_ones_f, b_src], writes=[b_ptr2])
        V(lambda e: e.tensor_copy(dst_ap, ptr2[:, 0:width]),
          reads=[b_ptr2], writes=[b_dst])

    def bcast_from_dram(dst_t, b_dst, src_dram, n, b0=0):
        for i in range(n):
            P.dma("sync", lnrow_s[:],
                  src_dram.ap().rearrange("(o d) -> o d", o=1)[0:1, b0 + i * 512:
                                                      b0 + (i + 1) * 512],
                  reads=[b_lnrow], writes=[b_lnrow])
            bcast_row(dst_t[:, i * 512:(i + 1) * 512], b_dst, lnrow_s[:],
                      b_lnrow, 512)

    bcast_from_dram(bc_a, b_bc_a, w_ln_w, 2)
    for i in range(2):
        P.dma("sync", lnrow_s[:],
              w_ln_b.ap().rearrange("(o d) -> o d", o=1)[0:1, i * 512:(i + 1) * 512],
              reads=[b_lnrow], writes=[b_lnrow])
        bcast_row(bc_a[:, D + i * 512:D + (i + 1) * 512], b_bc_a, lnrow_s[:],
                  b_lnrow, 512)
    bcast_from_dram(bc_rms, b_bc_rms, w_rms, 4)
    bcast_row(dcol_s[:], b_dcol, drow_s[:], b_drow, H)
    P.dma("sync", lfm_s[0:18, :], w_convb.ap().rearrange("(c p) -> c p", p=128),
          writes=[b_lfm])
    TE(lambda e: e.transpose(ptr2[:, 0:18], lfm_s[0:18, :], id_f[0:18, 0:18]),
       reads=[b_lfm, b_id_f], writes=[b_ptr2])
    V(lambda e: e.tensor_copy(convb_s[:], ptr2[:, 0:18]),
      reads=[b_ptr2], writes=[b_convb])
    P.dma("sync", lfm_s[:], w_mb1.ap().rearrange("(c p) -> c p", p=128),
          writes=[b_lfm])
    TE(lambda e: e.transpose(ptr2[:, 0:32], lfm_s[:], id_f[0:32, 0:32]),
       reads=[b_lfm, b_id_f], writes=[b_ptr2])
    V(lambda e: e.tensor_copy(mb1_t[:], ptr2[:, 0:32]),
      reads=[b_ptr2], writes=[b_mb1])
    P.dma("gpsimd", mb2_t[:], w_mb2.ap().rearrange("(o d) -> o d", o=1), writes=[b_mb2])
    for i in range(2):
        V(lambda e, i=i: e.memset(d2_t[i][0:1, :], 1.0), writes=[B_d2[i]])
        P.dma("sync", r2_t[i][1:2, :], ones_f_c.ap(), writes=[B_r2[i]])
    V(lambda e: e.memset(h_st[:], 0.0), writes=[b_h])
    G(lambda e: e.memset(h_bf[:], 0.0), writes=[b_hbf])
    V(lambda e: e.memset(tails_s[:], 0.0), writes=[b_tails])

    def ln_tile(src_ap, b_src, F, out_ap, b_out, affine=None, extra_reads=(),
                lane=0, out_bf=None):
        rd = [b_src] + list(extra_reads)
        bst = ST[lane]
        bsc = SCRB[lane]
        so = lane * 8
        sq = lane * 1024
        assert F <= 1024
        st = st_t[:, so:so + 8]
        V(lambda e: e.tensor_reduce(st[:, 0:1], src_ap, AX, ALU.add),
          reads=rd, writes=[bst])
        V(lambda e: e.scalar_tensor_tensor(scrb_t[:, sq:sq + F], src_ap, 1.0,
                                           src_ap, ALU.mult, ALU.mult,
                                           accum_out=st[:, 1:2]),
          reads=rd, writes=[bsc, bst])
        V(lambda e: e.tensor_scalar(st[:, 2:3], st[:, 0:1], 1.0 / F, None,
                                    ALU.mult),
          reads=[bst], writes=[bst])
        V(lambda e: e.tensor_tensor(st[:, 3:4], st[:, 2:3], st[:, 2:3],
                                    ALU.mult),
          reads=[bst], writes=[bst])
        V(lambda e: e.scalar_tensor_tensor(st[:, 3:4], st[:, 1:2], 1.0 / F,
                                           st[:, 3:4], ALU.mult, ALU.subtract),
          reads=[bst], writes=[bst])
        V(lambda e: e.tensor_scalar(st[:, 3:4], st[:, 3:4], 1e-5, None,
                                    ALU.add),
          reads=[bst], writes=[bst])
        S(lambda e: e.activation(st[:, 4:5], st[:, 3:4], AF.Sqrt),
          reads=[bst], writes=[bst])
        V(lambda e: e.reciprocal(st[:, 4:5], st[:, 4:5]),
          reads=[bst], writes=[bst])
        V(lambda e: e.scalar_tensor_tensor(st[:, 5:6], st[:, 2:3], -1.0,
                                           st[:, 4:5], ALU.mult, ALU.mult),
          reads=[bst], writes=[bst])
        V(lambda e: e.tensor_scalar(out_ap, src_ap, st[:, 4:5],
                                    st[:, 5:6], ALU.mult, ALU.add),
          reads=rd + [bst], writes=[b_out])
        if affine is not None:
            wap, bap, b_wb = affine
            V(lambda e: e.tensor_tensor(out_ap, out_ap, wap, ALU.mult),
              reads=[b_wb, b_out], writes=[b_out])
            if out_bf is not None:
                bfap, b_bf = out_bf
                V(lambda e: e.tensor_tensor(bfap, out_ap, bap, ALU.add),
                  reads=[b_wb, b_out], writes=[b_bf])
            else:
                V(lambda e: e.tensor_tensor(out_ap, out_ap, bap, ALU.add),
                  reads=[b_wb, b_out], writes=[b_out])

    def cast_group(src_ap, b_src, dst_t, b_dst, c0, cstride, d0, g, lane,
                   src_is_bf=False):
        bpt = PTRB[lane]
        po = lane * 512
        if src_is_bf:
            stage_ap, b_stage = src_ap, b_src
            soff = d0 * 128
        else:
            bsc = SCRB[lane]
            sq = lane * 1024
            V(lambda e, d0=d0, g=g: e.tensor_copy(
                scrb_t[:, sq:sq + g * 128],
                src_ap[:, d0 * 128:(d0 + g) * 128]),
              reads=[b_src], writes=[bsc])
            stage_ap, b_stage = scrb_t, bsc
            soff = sq
        for d in range(g):
            TE(lambda e, d=d: e.transpose(
                ptr[:, po + d * 128:po + (d + 1) * 128],
                stage_ap[:, soff + d * 128:soff + (d + 1) * 128], id_bf[:]),
               reads=[b_stage, b_id_bf], writes=[bpt])
        src_r = ptr[:, po:po + g * 128].rearrange("p (a c) -> p a c", c=128)
        dst_r = dst_t[:].rearrange("p (a c) -> p a c", c=cstride)[
            :, (c0 // cstride) + d0:(c0 // cstride) + d0 + g,
            (c0 % cstride):(c0 % cstride) + 128]
        V(lambda e, s=src_r, dd=dst_r: e.tensor_copy(dd, s),
          reads=[bpt], writes=[b_dst])

    def cast_transpose(src_ap, b_src, n, dst_t, b_dst, c0, cstride, lane=0,
                       src_is_bf=False):
        for d0 in range(0, n, 4):
            g = min(4, n - d0)
            cast_group(src_ap, b_src, dst_t, b_dst, c0, cstride, d0, g, lane,
                       src_is_bf)

    OPB = (0, 1, 4, 5)

    def fm_rhs(d):
        return xnfm_t[:, d * CH:(d + 1) * CH]

    def prep_units(ch):
        par = ch % 2
        t0 = ch * CH
        units = []

        def a_unit(t):
            def u(t=t):
                xb = B_x0[t % 2]
                P.dma("sync", x0_t[:, (t % 2) * D:(t % 2 + 1) * D],
                      x_in.ap()[t0 + t * 128: t0 + (t + 1) * 128, :],
                      writes=[xb])
                src = x0_t[:, (t % 2) * D:(t % 2 + 1) * D]
                ln_tile(src, xb, D, scr_t[:, 0:D], b_scr, lane=1)
                P.dma("sync", u1_d.ap()[t0 + t * 128: t0 + (t + 1) * 128, :],
                      scr_t[:, 0:D], reads=[b_scr], writes=[b_u1d])
                ln_tile(scr_t[:, 0:D], b_scr, D, scr_t[:, 0:D], b_scr,
                        affine=(bc_a[:, 0:D], bc_a[:, D:2 * D], b_bc_a),
                        lane=1, out_bf=(scrb_t[:, 1024:2048], b_scrbB))
                cast_transpose(scrb_t[:, 1024:2048], b_scrbB, 8, xnfm_t,
                               b_xnfm, t * 128, CH, lane=1, src_is_bf=True)
            return u

        for t in range(NB):
            units.append(a_unit(t))

        def ft_unit(ft):
            def u(ft=ft):
                fw = 128 if ft < 18 else 32
                c0 = 2048 + ft * 128
                wb = B_ipw[ft % 2]
                srcw = w_inproj.ap()[:, c0:c0 + fw].rearrange(
                    "(dc p) f -> p dc f", p=128)
                P.dma("gpsimd",
                      ipw_t[ft % 2][:, 0:8 * fw].rearrange(
                          "p (dc f) -> p dc f", f=fw),
                      srcw, writes=[wb])
                bi = 2 + ft % 2
                bank = B_pb[bi]
                for d in range(8):
                    TE(lambda e, ft=ft, d=d, fw=fw, bi=bi: e.matmul(
                        pb[bi][0:fw, 0:CH],
                        ipw_t[ft % 2][:, d * fw:(d + 1) * fw], fm_rhs(d),
                        start=(d == 0), stop=(d == 7)),
                       reads=[wb, b_xnfm], writes=[bank])
                if ft < 18:
                    rb = B_raw[ft % 2]
                    V(lambda e, ft=ft, bi=bi: e.tensor_copy(
                        raw_t[ft % 2][:, 3:3 + CH], pb[bi][:, 0:CH]),
                      reads=[bank], writes=[rb])
                    G(lambda e, ft=ft: e.tensor_copy(
                        raw_t[ft % 2][:, 0:3],
                        tails_s[:, ft * 3:(ft + 1) * 3]),
                      reads=[b_tails], writes=[rb])
                    G(lambda e, ft=ft: e.tensor_copy(
                        tails_s[:, ft * 3:(ft + 1) * 3],
                        raw_t[ft % 2][:, CH:CH + 3]),
                      reads=[rb], writes=[b_tails])
                    wc = convw_s
                    V(lambda e, ft=ft: e.tensor_scalar(
                        cvo_t[:], raw_t[ft % 2][:, 0:CH],
                        wc[:, ft * 4:ft * 4 + 1], convb_s[:, ft:ft + 1],
                        ALU.mult, ALU.add),
                      reads=[rb, b_convw, b_convb], writes=[b_cvo])
                    for k in range(1, 4):
                        V(lambda e, ft=ft, k=k: e.scalar_tensor_tensor(
                            cvo_t[:], raw_t[ft % 2][:, k:k + CH],
                            wc[:, ft * 4 + k:ft * 4 + k + 1], cvo_t[:],
                            ALU.mult, ALU.add),
                          reads=[rb, b_convw, b_cvo], writes=[b_cvo])
                    S(lambda e: e.activation(cvo_t[:], cvo_t[:], AF.Silu),
                      reads=[b_cvo], writes=[b_cvo])
                    if ft < 16:
                        cast_transpose(cvo_t[:], b_cvo, 4, xtm_t, b_xtm,
                                       ft * 128, DIN, lane=1)
                    elif ft == 16:
                        V(lambda e: e.tensor_copy(bmf_t[par][:], cvo_t[:]),
                          reads=[b_cvo], writes=[B_bmf[par]])
                    else:
                        V(lambda e: e.tensor_copy(cmf_t[par][:], cvo_t[:]),
                          reads=[b_cvo], writes=[B_cmf[par]])
                else:
                    V(lambda e, bi=bi: e.tensor_scalar(
                        sp_view, pb[bi][0:H, 0:CH], dtb_s[:], None, ALU.add),
                      reads=[bank, b_dtb], writes=[b_sp])
                    S(lambda e: e.activation(dt_fm[:], sp_view, AF.Abs),
                      reads=[b_sp], writes=[b_dtfm])
                    S(lambda e: e.activation(dt_fm[:], dt_fm[:], AF.Exp,
                                             scale=-1.0),
                      reads=[b_dtfm], writes=[b_dtfm])
                    V(lambda e: e.tensor_scalar(dt_fm[:], dt_fm[:], 1.0, None,
                                                ALU.add),
                      reads=[b_dtfm], writes=[b_dtfm])
                    S(lambda e: e.activation(dt_fm[:], dt_fm[:], AF.Ln),
                      reads=[b_dtfm], writes=[b_dtfm])
                    V(lambda e: e.tensor_scalar(sp_view, sp_view, 0.0, None,
                                                ALU.max),
                      reads=[b_sp], writes=[b_sp])
                    V(lambda e: e.tensor_tensor(dt_fm[:], dt_fm[:], sp_view,
                                                ALU.add),
                      reads=[b_sp, b_dtfm], writes=[b_dtfm])
                    V(lambda e: e.tensor_scalar(dta_fm[:], dt_fm[:], a_s[:],
                                                None, ALU.mult),
                      reads=[b_dtfm, b_a], writes=[b_dtafm])
                    S(lambda e: e.activation(lndt_fm[par][:], dt_fm[:], AF.Ln),
                      reads=[b_dtfm], writes=[B_lndt[par]])
            return u

        for ft in range(19):
            units.append(ft_unit(ft))

        def dtt_unit():
            def u():
                for b in range(NB):
                    for j, (srct, bsrc) in enumerate(
                            ((dta_fm, b_dtafm), (lndt_fm[par], B_lndt[par]))):
                        pc = ((2 * b + j) % 2) * H
                        TE(lambda e, b=b, srct=srct, bsrc=bsrc, pc=pc:
                           e.transpose(ptr2[:, pc:pc + H],
                                       srct[:, b * 128:(b + 1) * 128],
                                       id_f[0:H, 0:H]),
                           reads=[bsrc, b_id_f], writes=[b_ptr2])
                        V(lambda e, b=b, j=j, pc=pc: e.tensor_copy(
                            dtt_all[par][:, b * 2 * H + j * H:
                                         b * 2 * H + (j + 1) * H],
                            ptr2[:, pc:pc + H]),
                          reads=[b_ptr2], writes=[B_dtt[par]])
            return u

        units.append(dtt_unit())
        return units

    def z_part(ch):
        for zg in range(4):
            for hf in range(2):
                zb = B_zw[hf]
                srcw = w_inproj.ap()[hf * 512:(hf + 1) * 512,
                                     zg * 512:(zg + 1) * 512].rearrange(
                    "(dc p) f -> p dc f", p=128)
                P.dma("gpsimd",
                      zw_t[hf][:].rearrange("p (dc f) -> p dc f", f=512),
                      srcw, writes=[zb])
            for t in range(NB):
                bi = 2 + t % 2
                bank = B_pb[bi]
                for d in range(8):
                    zb = B_zw[d // 4]
                    TE(lambda e, t=t, d=d, bi=bi: e.matmul(
                        pb[bi][:, 0:512],
                        xnfm_t[:, d * CH + t * 128: d * CH + (t + 1) * 128],
                        zw_t[d // 4][:, (d % 4) * 512:(d % 4 + 1) * 512],
                        start=(d == 0), stop=(d == 7)),
                       reads=[zb, b_xnfm], writes=[bank])
                S(lambda e, zg=zg, t=t, bi=bi: e.activation(
                    sz_t[:, t * DIN + zg * 512: t * DIN + (zg + 1) * 512],
                    pb[bi][:, 0:512], AF.Silu),
                  reads=[bank], writes=[b_sz])

    def scan_block(ch, b, pend=None):
        par = ch % 2
        dtaT = dtt_all[par][:, b * 2 * H: b * 2 * H + H]
        lndtT = dtt_all[par][:, b * 2 * H + H: (b + 1) * 2 * H]
        bdt = B_dtt[par]
        bbmf = B_bmf[par]
        bcmf = B_cmf[par]
        TE(lambda e: e.matmul(ptr2[0:H, 0:128], dtaT, lt_s[:],
                              start=True, stop=True),
           reads=[bdt, b_lt], writes=[b_ptr2])
        V(lambda e: e.tensor_copy(lfm_s[:], ptr2[0:H, 0:128]),
          reads=[b_ptr2], writes=[b_lfm])
        TE(lambda e: e.matmul(ptr2[:, 0:H], lt_s[:], dtaT,
                              start=True, stop=True),
           reads=[bdt, b_lt], writes=[b_ptr2])
        S(lambda e: e.activation(expl_t[:], ptr2[:, 0:H], AF.Exp),
          reads=[b_ptr2], writes=[b_expl])
        V(lambda e: e.tensor_tensor(ctm_s[:], lndtT, ptr2[:, 0:H],
                                    ALU.subtract),
          reads=[bdt, b_ptr2], writes=[b_ctm])
        V(lambda e, b=b: e.tensor_tensor(
            cfm_s[:], lndt_fm[par][:, b * 128:(b + 1) * 128], lfm_s[:],
            ALU.subtract),
          reads=[B_lndt[par], b_lfm], writes=[b_cfm2])
        P.dma("sync", llrow_s[:], lfm_s[0:H, 127:128], reads=[b_lfm],
              writes=[b_llrow])
        TE(lambda e: e.matmul(ptr2[:, 0:H], ones_fs[0:1, 0:128], llrow_s[:],
                              start=True, stop=True),
           reads=[b_ones_f, b_llrow], writes=[b_ptr2])
        S(lambda e: e.activation(elast_t[:], ptr2[:, 0:H], AF.Exp),
          reads=[b_ptr2], writes=[b_elast])
        V(lambda e: e.tensor_tensor(ehat_t[:], ptr2[:, 0:H], ctm_s[:],
                                    ALU.add),
          reads=[b_ptr2, b_ctm], writes=[b_ehat])
        S(lambda e: e.activation(ehat_t[:], ehat_t[:], AF.Exp),
          reads=[b_ehat], writes=[b_ehat])
        TE(lambda e, b=b: e.transpose(ptr[:, 0:128],
                                      bmf_t[par][:, b * 128:(b + 1) * 128],
                                      id_bf[:]),
           reads=[bbmf, b_id_bf], writes=[b_ptr])
        V(lambda e: e.tensor_copy(bmt_t[:], ptr[:, 0:128]),
          reads=[b_ptr], writes=[b_bmt])
        for i in range(4):
            TE(lambda e, b=b, i=i: e.matmul(
                pb[2][:, i * 128:(i + 1) * 128],
                bmf_t[par][:, b * 128:(b + 1) * 128],
                cmf_t[par][:, b * 128:(b + 1) * 128],
                start=True, stop=True),
               reads=[bbmf, bcmf], writes=[B_pb[2]])

        def emit_difg(g):
            gb = g % 2
            P.dma("sync", r2_t[gb][0:1, :], lfm_s[4 * g:4 * g + 4, :],
                  reads=[b_lfm], writes=[B_r2[gb]])
            P.dma("sync", d2_t[gb][1:2, :], cfm_s[4 * g:4 * g + 4, :],
                  reads=[b_cfm2], writes=[B_d2[gb]])
            for i in range(4):
                TE(lambda e, gb=gb, i=i: e.matmul(
                    pb[gb][:, i * 128:(i + 1) * 128],
                    d2_t[gb][:, i * 128:(i + 1) * 128],
                    r2_t[gb][:, i * 128:(i + 1) * 128],
                    start=True, stop=True),
                   reads=[B_d2[gb], B_r2[gb]], writes=[B_pb[gb]])

        def emit_w(g):
            gb = g % 2
            V(lambda e, gb=gb: e.tensor_tensor(earg_t[:], pb[gb][:],
                                               mask4_s[:], ALU.add),
              reads=[B_pb[gb], b_mask4], writes=[b_earg])
            S(lambda e: e.activation(earg_t[:], earg_t[:], AF.Exp),
              reads=[b_earg], writes=[b_earg])
            V(lambda e: e.tensor_tensor(w4_t[gb][:], earg_t[:],
                                        pb[2][:], ALU.mult),
              reads=[b_earg, B_pb[2]], writes=[B_w4[gb]])

        def emit_ys(g):
            gb = g % 2
            for i in range(4):
                h = 4 * g + i
                TE(lambda e, b=b, gb=gb, h=h, i=i: e.matmul(
                    pb[4][:, gb * 256 + i * 64: gb * 256 + (i + 1) * 64],
                    w4_t[gb][:, i * 128:(i + 1) * 128],
                    xtm_t[:, b * DIN + h * HD: b * DIN + (h + 1) * HD],
                    start=True, stop=True),
                   reads=[B_w4[gb], b_xtm], writes=[B_pb[4]])
                TE(lambda e, b=b, gb=gb, h=h, i=i: e.matmul(
                    pb[5][:, gb * 256 + i * 64: gb * 256 + (i + 1) * 64],
                    cmf_t[par][:, b * 128:(b + 1) * 128],
                    h_bf[:, h * HD:(h + 1) * HD],
                    start=True, stop=True),
                   reads=[bcmf, b_hbf], writes=[B_pb[5]])
            for i in range(4):
                h = 4 * g + i
                V(lambda e, h=h, i=i: e.tensor_scalar(
                    bht4_t[:, i * 128:(i + 1) * 128],
                    bmt_t[:], ehat_t[:, h:h + 1], None, ALU.mult),
                  reads=[b_bmt, b_ehat], writes=[b_bht])
            for i in range(4):
                h = 4 * g + i
                TE(lambda e, b=b, gb=gb, h=h, i=i: e.matmul(
                    ptr2[:, gb * 256 + i * 64: gb * 256 + (i + 1) * 64],
                    bht4_t[:, i * 128:(i + 1) * 128],
                    xtm_t[:, b * DIN + h * HD: b * DIN + (h + 1) * HD],
                    start=True, stop=True),
                   reads=[b_bht, b_xtm], writes=[b_ptr2])

        def emit_epi(g):
            gb = g % 2
            V(lambda e, gb=gb: e.tensor_copy(
                yintra_t[:], pb[4][:, gb * 256:(gb + 1) * 256]),
              reads=[B_pb[4]], writes=[b_yintra])
            for i in range(4):
                h = 4 * g + i
                V(lambda e, gb=gb, h=h, i=i: e.scalar_tensor_tensor(
                    y_blk[:, h * HD:(h + 1) * HD],
                    pb[5][:, gb * 256 + i * 64: gb * 256 + (i + 1) * 64],
                    expl_t[:, h:h + 1],
                    yintra_t[:, i * 64:(i + 1) * 64],
                    ALU.mult, ALU.add),
                  reads=[B_pb[5], b_expl, b_yintra], writes=[b_yblk])
                V(lambda e, b=b, h=h: e.scalar_tensor_tensor(
                    y_blk[:, h * HD:(h + 1) * HD],
                    xtm_t[:, b * DIN + h * HD: b * DIN + (h + 1) * HD],
                    dcol_s[:, h:h + 1],
                    y_blk[:, h * HD:(h + 1) * HD],
                    ALU.mult, ALU.add),
                  reads=[b_xtm, b_dcol, b_yblk], writes=[b_yblk])
                V(lambda e, gb=gb, h=h, i=i: e.scalar_tensor_tensor(
                    h_st[:, h * HD:(h + 1) * HD],
                    h_st[:, h * HD:(h + 1) * HD],
                    elast_t[:, h:h + 1],
                    ptr2[:, gb * 256 + i * 64: gb * 256 + (i + 1) * 64],
                    ALU.mult, ALU.add),
                  reads=[b_elast, b_ptr2], writes=[b_h])

        emit_difg(0)
        emit_w(0)
        if pend is not None:
            pend[0]()
        emit_difg(1)
        for g in range(8):
            emit_ys(g)
            if g + 1 < 8:
                emit_w(g + 1)
            if g + 2 < 8:
                emit_difg(g + 2)
            if pend is not None and g % 2 == 0:
                pend[1][g // 2]()
            emit_epi(g)
        G(lambda e: e.tensor_copy(h_bf[:], h_st[:]), reads=[b_h],
          writes=[b_hbf])


        def chain(b=b):
            G(lambda e: e.tensor_tensor(y_blk[:], y_blk[:],
                                        sz_t[:, b * DIN:(b + 1) * DIN],
                                        ALU.mult),
              reads=[b_sz, b_yblk], writes=[b_yblk])
            V(lambda e: e.scalar_tensor_tensor(scrb_t[:], y_blk[:], 1.0,
                                               y_blk[:], ALU.mult, ALU.mult,
                                               accum_out=rms_r[:, 0:1]),
              reads=[b_yblk], writes=[b_scrb, b_scrbB, b_rmsr])
            V(lambda e: e.tensor_scalar(rms_r[:, 1:2], rms_r[:, 0:1],
                                        1.0 / DIN, 1e-5, ALU.mult, ALU.add),
              reads=[b_rmsr], writes=[b_rmsr])
            S(lambda e: e.activation(rms_r[:, 1:2], rms_r[:, 1:2], AF.Ln),
              reads=[b_rmsr], writes=[b_rmsr])
            V(lambda e: e.tensor_scalar(rms_r[:, 1:2], rms_r[:, 1:2], -0.5,
                                        None, ALU.mult),
              reads=[b_rmsr], writes=[b_rmsr])
            S(lambda e: e.activation(rms_r[:, 1:2], rms_r[:, 1:2], AF.Exp),
              reads=[b_rmsr], writes=[b_rmsr])
            V(lambda e: e.tensor_scalar(y_blk[:], y_blk[:], rms_r[:, 1:2],
                                        None, ALU.mult),
              reads=[b_rmsr, b_yblk], writes=[b_yblk])
            G(lambda e: e.tensor_tensor(y_blk[:], y_blk[:], bc_rms[:],
                                        ALU.mult),
              reads=[b_bc_rms, b_yblk], writes=[b_yblk])

        casts = []
        for grp in range(4):
            def cg(grp=grp, b=b):
                cast_group(y_blk[:], b_yblk, ynfm_t, b_ynfm, b * 128, CH,
                           grp * 4, 4, 0)
            casts.append(cg)
        return (chain, casts)

    def post_units(ch):
        t0 = ch * CH
        units = []

        def op_half_unit(half):
            def u(half=half):
                for t in range(2):
                    P.dma("sync", u1q_t[:, (t % 2) * 512:(t % 2 + 1) * 512],
                          u1_d.ap()[t0 + t * 128: t0 + (t + 1) * 128,
                                    half * 512:(half + 1) * 512],
                          reads=[b_u1d], writes=[B_u1q[t % 2]])
                for k in range(16):
                    wb = B_opw[k % 2]
                    src = w_outp.ap()[k * 128:(k + 1) * 128,
                                      half * 512:(half + 1) * 512]
                    P.dma("gpsimd", opw_t[k % 2][:], src, writes=[wb])
                    for t in range(NB):
                        bi = OPB[t]
                        TE(lambda e, k=k, t=t, bi=bi: e.matmul(
                            pb[bi][:, 0:512],
                            ynfm_t[:, k * CH + t * 128: k * CH + (t + 1) * 128],
                            opw_t[k % 2][:],
                            start=(k == 0), stop=(k == 15)),
                           reads=[wb, b_ynfm], writes=[B_pb[bi]])
                for t in range(NB):
                    bi = OPB[t]
                    V(lambda e, t=t, half=half, bi=bi: e.tensor_tensor(
                        out1_t[:, t * D + half * 512: t * D + (half + 1) * 512],
                        pb[bi][:, 0:512],
                        u1q_t[:, (t % 2) * 512:(t % 2 + 1) * 512],
                        ALU.add),
                      reads=[B_pb[bi], B_u1q[t % 2]], writes=[b_out1])
                    if t + 2 < NB:
                        P.dma("sync",
                              u1q_t[:, (t % 2) * 512:(t % 2 + 1) * 512],
                              u1_d.ap()[t0 + (t + 2) * 128:
                                        t0 + (t + 3) * 128,
                                        half * 512:(half + 1) * 512],
                              reads=[b_u1d], writes=[B_u1q[t % 2]])
            return u

        units.append(op_half_unit(0))
        units.append(op_half_unit(1))

        def mlp_ln_unit(t):
            def u(t=t):
                ln_tile(out1_t[:, t * D:(t + 1) * D], b_out1, D,
                        y_blk[:, 0:D], b_yblk,
                        affine=(bc_a[:, 0:D], bc_a[:, D:2 * D], b_bc_a),
                        lane=0, out_bf=(scrb_t[:, 0:1024], b_scrb))
                cast_transpose(scrb_t[:, 0:1024], b_scrb, 8, mlpfm_t, b_mlpfm,
                               t * 128, CH, lane=0, src_is_bf=True)
            return u

        for t in range(NB):
            units.append(mlp_ln_unit(t))

        def mlp_fc_unit(fc):
            def u(fc=fc):
                c0 = fc * 1024
                for f in range(8):
                    fb = B_zw[f % 2]
                    src1 = w_m1.ap()[:, c0 + f * 128:
                                     c0 + (f + 1) * 128].rearrange(
                        "(dc p) f -> p dc f", p=128)
                    P.dma("gpsimd",
                          zw_t[f % 2][:, 0:1024].rearrange(
                              "p (dc f) -> p dc f", f=128),
                          src1, writes=[fb])
                    bi = 4 + f % 2
                    bank = B_pb[bi]
                    for d in range(8):
                        TE(lambda e, f=f, d=d, bi=bi: e.matmul(
                            pb[bi][:, 0:512],
                            zw_t[f % 2][:, d * 128:(d + 1) * 128],
                            mlpfm_t[:, d * CH:(d + 1) * CH],
                            start=(d == 0), stop=(d == 7)),
                           reads=[fb, b_mlpfm], writes=[bank])
                    idx = fc * 8 + f
                    S(lambda e, f=f, idx=idx, bi=bi: e.activation(
                        hr_t[:, f * 512:(f + 1) * 512], pb[bi][:, 0:512],
                        AF.Relu, bias=mb1_t[:, idx:idx + 1]),
                      reads=[bank, b_mb1], writes=[b_hr])
                last = (fc == 3)
                for half in range(2):
                    for k in range(8):
                        w2b = B_mw2[k % 2]
                        src2 = w_m2.ap()[c0 + k * 128:c0 + (k + 1) * 128,
                                         half * 512:(half + 1) * 512]
                        P.dma("gpsimd", mw2_t[k % 2][:], src2, writes=[w2b])
                        for t in range(NB):
                            bi = OPB[t]
                            TE(lambda e, k=k, t=t, bi=bi: e.matmul(
                                pb[bi][:, 0:512],
                                hr_t[:, k * 512 + t * 128:
                                     k * 512 + (t + 1) * 128],
                                mw2_t[k % 2][:],
                                start=(k == 0), stop=(k == 7 and not last)),
                               reads=[w2b, b_hr], writes=[B_pb[bi]])
                    if last:
                        for t in range(NB):
                            bi = OPB[t]
                            TE(lambda e, t=t, half=half, bi=bi: e.matmul(
                                pb[bi][:, 0:512], ones_bs[0:1, 0:128],
                                mb2_t[0:1, half * 512:(half + 1) * 512],
                                start=False, stop=True),
                               reads=[b_ones_b, b_mb2], writes=[B_pb[bi]])
                    for t in range(NB):
                        bi = OPB[t]
                        V(lambda e, t=t, half=half, bi=bi: e.tensor_tensor(
                            out1_t[:, t * D + half * 512:
                                   t * D + (half + 1) * 512],
                            pb[bi][:, 0:512],
                            out1_t[:, t * D + half * 512:
                                   t * D + (half + 1) * 512],
                            ALU.add),
                          reads=[B_pb[bi], b_out1], writes=[b_out1])
            return u

        for fc in range(4):
            units.append(mlp_fc_unit(fc))

        def g_unit(t):
            def u(t=t):
                xb = B_x0[t % 2]
                P.dma("sync", x0_t[:, (t % 2) * D:(t % 2 + 1) * D],
                      x_in.ap()[t0 + t * 128: t0 + (t + 1) * 128, :],
                      writes=[xb])
                ln_tile(out1_t[:, t * D:(t + 1) * D], b_out1, D,
                        y_blk[:, 0:D], b_yblk, lane=0)
                V(lambda e, t=t: e.tensor_tensor(
                    x0_t[:, (t % 2) * D:(t % 2 + 1) * D], y_blk[:, 0:D],
                    x0_t[:, (t % 2) * D:(t % 2 + 1) * D], ALU.add),
                  reads=[b_yblk, xb], writes=[xb])
                P.dma("sync", x1_d.ap()[t0 + t * 128: t0 + (t + 1) * 128, :],
                      x0_t[:, (t % 2) * D:(t % 2 + 1) * D], reads=[xb],
                      writes=[b_x1d])
                if stage == 1:
                    P.dma("sync",
                          y_out.ap()[t0 + t * 128: t0 + (t + 1) * 128, :],
                          x0_t[:, (t % 2) * D:(t % 2 + 1) * D], reads=[xb])
            return u

        for t in range(NB):
            units.append(g_unit(t))
        return units

    def interleave(pa, pb_):
        na, nb = len(pa), len(pb_)
        ib = 0
        for ia, uu in enumerate(pa):
            uu()
            want = int(nb * (ia + 1) / na)
            while ib < want:
                pb_[ib]()
                ib += 1
        while ib < nb:
            pb_[ib]()
            ib += 1

    for uu in prep_units(0):
        uu()
    for ch in range(NCH):
        z_part(ch)
        pend = None
        for b in range(NB):
            pend = scan_block(ch, b, pend)
        pend[0]()
        for cg in pend[1]:
            cg()
        pu = prep_units(ch + 1) if ch + 1 < NCH else []
        interleave(post_units(ch), pu)

    if stage >= 2:
        P.fence()
        bcast_from_dram(bc_a, b_bc_a, w_l1g, 2)
        for i in range(2):
            P.dma("sync", lnrow_s[:],
                  w_l1b.ap().rearrange("(o d) -> o d", o=1)[0:1,
                                                           i * 512:(i + 1) * 512],
                  reads=[b_lnrow], writes=[b_lnrow])
            bcast_row(bc_a[:, D + i * 512:D + (i + 1) * 512], b_bc_a, lnrow_s[:],
                      b_lnrow, 512)
        V(lambda e: e.memset(
            vsb_t[:].rearrange("p (a c) -> p a c", c=65)[:, :, 64:65], 1.0),
          writes=[b_vsb])
        for ch in range(NCH):
            t0 = ch * CH
            for t in range(NB):
                xb = B_x0[t % 2]
                P.dma("sync", x0_t[:, (t % 2) * D:(t % 2 + 1) * D],
                      x1_d.ap()[t0 + t * 128: t0 + (t + 1) * 128, :],
                      reads=[b_x1d], writes=[xb])
                src_ap = x0_t[:, (t % 2) * D:(t % 2 + 1) * D]
                ln_tile(src_ap, xb, D, scr_t[:, 0:D], b_scr)
                ln_tile(scr_t[:, 0:D], b_scr, D, scr_t[:, 0:D], b_scr,
                        affine=(bc_a[:, 0:D], bc_a[:, D:2 * D], b_bc_a))
                cast_transpose(scr_t[:, 0:D], b_scr, 8, xnfm_t, b_xnfm,
                               t * 128, CH)
            for part, dst, bdst in ((1, kfm_t, b_kfm), (0, None, None)):
                for ftl in range(8):
                    c0 = part * D + ftl * 128
                    wb = B_ipw[ftl % 2]
                    srcw = w_qkv.ap()[:, c0:c0 + 128].rearrange(
                        "(dc p) f -> p dc f", p=128)
                    P.dma("gpsimd",
                          ipw_t[ftl % 2][:].rearrange("p (dc f) -> p dc f", f=128),
                          srcw, writes=[wb])
                    bank = B_pb[ftl % 2]
                    for d in range(8):
                        TE(lambda e, ftl=ftl, d=d: e.matmul(
                            pb[ftl % 2][:, 0:CH],
                            ipw_t[ftl % 2][:, d * 128:(d + 1) * 128],
                            xnfm_t[:, d * CH:(d + 1) * CH],
                            start=(d == 0), stop=(d == 7)),
                           reads=[wb, b_xnfm], writes=[bank])
                    if part == 1:
                        V(lambda e, ftl=ftl, ch=ch: e.tensor_copy(
                            kfm_t[:, ftl * T + ch * CH: ftl * T + (ch + 1) * CH],
                            pb[ftl % 2][:, 0:CH]),
                          reads=[bank], writes=[b_kfm])
                    else:
                        V(lambda e, ftl=ftl: e.tensor_copy(
                            scrb_t[:, 0:CH], pb[ftl % 2][:, 0:CH]),
                          reads=[bank], writes=[b_scrb])
                        P.dma("sync",
                              q_d.ap()[ftl, :, ch * CH:(ch + 1) * CH],
                              scrb_t[:, 0:CH], reads=[b_scrb], writes=[b_qd])
            for half in range(2):
                for dh in range(2):
                    src_vw = w_qkv.ap()[dh * 512:(dh + 1) * 512,
                                        2 * D + half * 512:
                                        2 * D + (half + 1) * 512].rearrange(
                        "(dc p) f -> p dc f", p=128)
                    P.dma("gpsimd",
                          vw_t[:].rearrange("p (dc f) -> p dc f", f=512),
                          src_vw, writes=[b_vw])
                    for t in range(NB):
                        for d in range(4):
                            TE(lambda e, d=d, dh=dh, t=t: e.matmul(
                                pb[t][:, 0:512],
                                xnfm_t[:, (dh * 4 + d) * CH + t * 128:
                                       (dh * 4 + d) * CH + (t + 1) * 128],
                                vw_t[:, d * 512:(d + 1) * 512],
                                start=(dh == 0 and d == 0),
                                stop=(dh == 1 and d == 3)),
                               reads=[b_vw, b_xnfm], writes=[B_pb[t]])
                for t in range(NB):
                    kt = ch * NB + t
                    dst = vsb_t[:, kt * 1040 + half * 8 * 65:
                                kt * 1040 + (half + 1) * 8 * 65] \
                        .rearrange("p (a c) -> p a c", c=65)[:, :, 0:64]
                    srcp = pb[t][:, 0:512].rearrange("p (a c) -> p a c", c=64)
                    V(lambda e, dst=dst, srcp=srcp, t=t: e.tensor_copy(
                        dst, srcp),
                      reads=[B_pb[t]], writes=[b_vsb])

    if stage >= 2:
        P.fence()
        negc_s, b_negc = sb("negc_s", [128, 1], F32)
        V(lambda e: e.memset(negc_s[:], -ATT_C), writes=[b_negc])
        sc_pool = (pb[0], pb[1], ptr2)
        sc_bufs = (B_pb[0], B_pb[1], b_ptr2)
        sc_ctr = [0]
        P.dma("sync", mask4_s[:], amask_c.ap(), reads=[b_mask4],
              writes=[b_mask4])
        for qg in range(8):
            nk = 4 * qg + 4
            pend_ep = []
            for p in range(8):
                qb = B_qsb[p % 2]
                P.dma("sync", qsb_t[p % 2][:],
                      q_d.ap()[p, :, qg * 512:(qg + 1) * 512],
                      reads=[b_qd], writes=[qb])
                AB = (4, 5) if p % 2 == 0 else (2, 3)

                def emit_scores(hi, kb, p=p, qb=qb):
                    h = 2 * p + hi
                    hh = hi * 64
                    pt_i = 2 * hi + kb % 2
                    diag = kb >= 4 * qg
                    off = (kb - 4 * qg) * 128 if diag else 0
                    wdt = 512 - off
                    slot = sc_ctr[0] % 3
                    sc_ctr[0] += 1
                    sps = sc_pool[slot]
                    sbank = sc_bufs[slot]
                    TE(lambda e, h=h, hh=hh, kb=kb, off=off, wdt=wdt,
                       sps=sps: e.matmul(
                           sps[:, 0:wdt],
                           kfm_t[hh:hh + 64, p * T + kb * 128:
                                 p * T + (kb + 1) * 128],
                           qsb_t[p % 2][hh:hh + 64, off:512],
                           start=True, stop=True),
                       reads=[b_kfm, qb], writes=[sbank])
                    if diag:
                        V(lambda e, sps=sps, wdt=wdt: e.tensor_tensor(
                            rc_t[:, 0:wdt],
                            sps[:, 0:wdt], mask4_s[:, 0:wdt], ALU.add),
                          reads=[sbank, b_mask4], writes=[b_rc])
                        S(lambda e, pt_i=pt_i, off=off, wdt=wdt: e.activation(
                            pt_t[pt_i][:, off:512], rc_t[:, 0:wdt], AF.Exp,
                            bias=negc_s[:], scale=0.125),
                          reads=[b_rc, b_negc], writes=[B_pt[pt_i]])
                    else:
                        S(lambda e, pt_i=pt_i, sps=sps, wdt=wdt:
                          e.activation(
                              pt_t[pt_i][:, 0:512], sps[:, 0:wdt],
                              AF.Exp, bias=negc_s[:], scale=0.125),
                          reads=[sbank, b_negc], writes=[B_pt[pt_i]])

                def emit_pv(hi, kb, p=p, AB=AB):
                    h = 2 * p + hi
                    ob = AB[hi]
                    pt_i = 2 * hi + kb % 2
                    diag = kb >= 4 * qg
                    off = (kb - 4 * qg) * 128 if diag else 0
                    TE(lambda e, h=h, kb=kb, off=off, ob=ob, pt_i=pt_i:
                       e.matmul(
                           pb[ob][0:65, off:512],
                           vsb_t[:, kb * 1040 + h * 65:
                                 kb * 1040 + (h + 1) * 65],
                           pt_t[pt_i][:, off:512],
                           start=(kb == 0), stop=(kb == nk - 1)),
                       reads=[b_vsb, B_pt[pt_i]], writes=[B_pb[ob]])

                emit_scores(0, 0)
                emit_scores(1, 0)
                for ep in pend_ep:
                    ep()
                pend_ep = []
                for kb in range(1, nk):
                    emit_scores(0, kb)
                    emit_pv(0, kb - 1)
                    emit_scores(1, kb)
                    emit_pv(1, kb - 1)
                emit_pv(0, nk - 1)
                emit_pv(1, nk - 1)

                def mk_ep(hi, p=p, AB=AB):
                    def ep():
                        ob = AB[hi]
                        hh = hi * 64
                        V(lambda e, ob=ob: e.reciprocal(
                            rl_s[:], pb[ob][64:65, 0:512]),
                          reads=[B_pb[ob]], writes=[b_rl])
                        slot = sc_ctr[0] % 3
                        sc_ctr[0] += 1
                        sps = sc_pool[slot]
                        sbank = sc_bufs[slot]
                        TE(lambda e, sps=sps: e.matmul(sps[:, 0:512],
                                              ones_fs[0:1, 0:128],
                                              rl_s[:], start=True, stop=True),
                           reads=[b_ones_f, b_rl], writes=[sbank])
                        S(lambda e, sps=sps: e.activation(rc_t, sps[:, 0:512],
                                                 AF.Copy),
                          reads=[sbank], writes=[b_rc])
                        V(lambda e, hh=hh, ob=ob, p=p: e.tensor_tensor(
                            hr_t[hh:hh + 64, p * 512:(p + 1) * 512],
                            pb[ob][0:64, 0:512], rc_t[0:64, 0:512], ALU.mult),
                          reads=[B_pb[ob], b_rc], writes=[b_hr])
                    return ep

                pend_ep = [mk_ep(0), mk_ep(1)]
            for ep in pend_ep:
                ep()
            pend_ep = []
            for tp in range(2):
                for half in range(2):
                    for k in range(8):
                        wb = B_wst[k % 2]
                        P.dma("gpsimd", wst_t[k % 2][:],
                              w_wo.ap()[k * 128:(k + 1) * 128,
                                        half * 512:(half + 1) * 512],
                              writes=[wb])
                        for ti in range(2):
                            t = 2 * tp + ti
                            TE(lambda e, k=k, t=t, ti=ti, half=half: e.matmul(
                                pb[ti * 2 + half][:, 0:512],
                                hr_t[:, k * 512 + t * 128:
                                     k * 512 + (t + 1) * 128],
                                wst_t[k % 2][:],
                                start=(k == 0), stop=(k == 7)),
                               reads=[wb, b_hr], writes=[B_pb[ti * 2 + half]])
                for ti in range(2):
                    t = 2 * tp + ti
                    xb = B_x0[t % 2]
                    tg = qg * 512 + t * 128
                    P.dma("sync", x0_t[:, (t % 2) * D:(t % 2 + 1) * D],
                          x1_d.ap()[tg:tg + 128, :], reads=[b_x1d], writes=[xb])
                    ln_tile(x0_t[:, (t % 2) * D:(t % 2 + 1) * D], xb, D,
                            scr_t[:, 0:D], b_scr)
                    for half in range(2):
                        V(lambda e, ti=ti, half=half: e.tensor_tensor(
                            scr_t[:, half * 512:(half + 1) * 512],
                            pb[ti * 2 + half][:, 0:512],
                            scr_t[:, half * 512:(half + 1) * 512], ALU.add),
                          reads=[B_pb[ti * 2 + half], b_scr], writes=[b_scr])
                    P.dma("sync", xres_d.ap()[tg:tg + 128, :], scr_t[:, 0:D],
                          reads=[b_scr], writes=[b_xresd])
                    if stage == 2:
                        P.dma("sync", y_out.ap()[tg:tg + 128, :], scr_t[:, 0:D],
                              reads=[b_scr])

    if stage >= 3:
        P.fence()
        bcast_from_dram(bc_a, b_bc_a, w_l2g, 2)
        for i in range(2):
            P.dma("sync", lnrow_s[:],
                  w_l2b.ap().rearrange("(o d) -> o d", o=1)[0:1,
                                                           i * 512:(i + 1) * 512],
                  reads=[b_lnrow], writes=[b_lnrow])
            bcast_row(bc_a[:, D + i * 512:D + (i + 1) * 512], b_bc_a, lnrow_s[:],
                      b_lnrow, 512)
        P.dma("gpsimd",
              gw_s[:].rearrange("p (dc f) -> p dc f", f=8),
              w_gate.ap().rearrange("(dc p) f -> p dc f", p=128), writes=[b_gw])
        def yg_view(t):
            return gates_t[:, t * NE:(t + 1) * NE]

        for half in range(2):
            t0 = half * 2048
            if half == 0:
                for ex in range(NE):
                    P.dma("sync", lfm_s[:],
                          w_eb1.ap()[ex, :].rearrange("(c p) -> c p", p=128),
                          reads=[b_lfm], writes=[b_lfm])
                    TE(lambda e: e.transpose(ptr2[:, 0:32], lfm_s[:],
                                             id_f[0:32, 0:32]),
                       reads=[b_lfm, b_id_f], writes=[b_ptr2])
                    V(lambda e, ex=ex: e.tensor_copy(
                        mb1a_t[:, ex * 32:(ex + 1) * 32], ptr2[:, 0:32]),
                      reads=[b_ptr2], writes=[b_mb1a])

            def emit_wdma(s):
                ex, fc = divmod(s, 8)
                par = s % 2
                c0 = fc * 512
                for kh in range(2):
                    P.dma("gpsimd",
                          ew2_t[par][:, kh * 2 * D:(kh + 1) * 2 * D].rearrange(
                              "p (kc d) -> p kc d", d=D),
                          w_e2.ap()[ex, c0 + kh * 256:c0 + (kh + 1) * 256,
                                    :].rearrange("(kc p) d -> p kc d", p=128),
                          writes=[B_ew2[par][kh]])
                for f in range(4):
                    srcw = w_e1.ap()[ex, :,
                                     c0 + f * 128:c0 + (f + 1) * 128] \
                        .rearrange("(dc p) f -> p dc f", p=128)
                    P.dma("gpsimd",
                          ew1_t[par][:, f * 1024:(f + 1) * 1024].rearrange(
                              "p (dc c) -> p dc c", c=128),
                          srcw, writes=[B_ew1[par][f]])
                if fc == 0:
                    P.dma("gpsimd", mb2a_t[ex % 2][:],
                          w_eb2.ap()[ex, :].rearrange("(o d) -> o d", o=1),
                          writes=[B_mb2a[ex % 2]])

            emit_wdma(0)
            for t in range(16):
                lane = t % 2
                st = st_t[:, lane * 8:lane * 8 + 8]
                sc = st_t[:, 16 + lane * 16:32 + lane * 16]
                bst = ST[lane]
                bi = 4 + lane
                xb = B_x0[t % 2]
                P.dma("sync", x0_t[:, (t % 2) * D:(t % 2 + 1) * D],
                      xres_d.ap()[t0 + t * 128: t0 + (t + 1) * 128, :],
                      reads=[b_xresd], writes=[xb])
                V(lambda e, t=t: e.tensor_copy(
                    acc_t[:, t * D:(t + 1) * D],
                    x0_t[:, (t % 2) * D:(t % 2 + 1) * D]),
                  reads=[xb], writes=[b_acc])
                ln_tile(x0_t[:, (t % 2) * D:(t % 2 + 1) * D], xb, D,
                        scr_t[:, 0:D], b_scr,
                        affine=(bc_a[:, 0:D], bc_a[:, D:2 * D], b_bc_a),
                        lane=lane,
                        out_bf=(scrb_t[:, lane * 1024:(lane + 1) * 1024],
                                SCRB[lane]))
                cast_transpose(scrb_t[:, lane * 1024:(lane + 1) * 1024],
                               SCRB[lane], 8, mfm_t, b_mfm,
                               t * 128, 2048, lane=lane, src_is_bf=True)
                for d in range(8):
                    TE(lambda e, t=t, d=d, bi=bi: e.matmul(
                        pb[bi][:, 0:NE],
                        mfm_t[:, d * 2048 + t * 128: d * 2048 + (t + 1) * 128],
                        gw_s[:, d * 8:(d + 1) * 8],
                        start=(d == 0), stop=(d == 7)),
                       reads=[b_mfm, b_gw], writes=[B_pb[bi]])
                V(lambda e, bi=bi, st=st: e.tensor_reduce(
                    st[:, 0:1], pb[bi][:, 0:NE], AX, ALU.max),
                  reads=[B_pb[bi]], writes=[bst])
                V(lambda e, st=st: e.tensor_scalar(st[:, 0:1], st[:, 0:1],
                                                   -1.0, None, ALU.mult),
                  reads=[bst], writes=[bst])
                S(lambda e, t=t, bi=bi, st=st: e.activation(
                    yg_view(t), pb[bi][:, 0:NE], AF.Exp, bias=st[:, 0:1]),
                  reads=[B_pb[bi], bst], writes=[b_gates])
                V(lambda e, t=t, st=st: e.tensor_reduce(
                    st[:, 1:2], yg_view(t), AX, ALU.max),
                  reads=[b_gates], writes=[bst])
                V(lambda e, st=st: e.tensor_scalar(st[:, 2:3], st[:, 1:2],
                                                   0.999999, None, ALU.mult),
                  reads=[bst], writes=[bst])
                V(lambda e, t=t, st=st, sc=sc: e.tensor_scalar(
                    sc[:, 0:NE], yg_view(t), st[:, 2:3], None, ALU.is_ge),
                  reads=[b_gates, bst], writes=[bst])
                V(lambda e, t=t, sc=sc: e.scalar_tensor_tensor(
                    sc[:, 8:16], sc[:, 0:NE], -1e30, yg_view(t),
                    ALU.mult, ALU.add),
                  reads=[bst, b_gates], writes=[bst])
                V(lambda e, st=st, sc=sc: e.tensor_reduce(
                    st[:, 3:4], sc[:, 8:16], AX, ALU.max),
                  reads=[bst], writes=[bst])
                V(lambda e, st=st: e.tensor_scalar(st[:, 4:5], st[:, 3:4],
                                                   0.999999, None, ALU.mult),
                  reads=[bst], writes=[bst])
                V(lambda e, t=t, st=st, sc=sc: e.tensor_scalar(
                    sc[:, 0:NE], yg_view(t), st[:, 4:5], None, ALU.is_ge),
                  reads=[b_gates, bst], writes=[bst])
                V(lambda e, st=st: e.tensor_tensor(st[:, 5:6], st[:, 1:2],
                                                   st[:, 3:4], ALU.add),
                  reads=[bst], writes=[bst])
                V(lambda e, st=st: e.reciprocal(st[:, 5:6], st[:, 5:6]),
                  reads=[bst], writes=[bst])
                V(lambda e, t=t, sc=sc: e.tensor_tensor(
                    yg_view(t), yg_view(t), sc[:, 0:NE], ALU.mult),
                  reads=[bst, b_gates], writes=[b_gates])
                V(lambda e, t=t, st=st: e.tensor_scalar(
                    yg_view(t), yg_view(t), st[:, 5:6], None, ALU.mult),
                  reads=[bst, b_gates], writes=[b_gates])
            NS = NE * 8
            ci = [0]
            for s in range(NS):
                ex, fc = divmod(s, 8)
                par = s % 2
                if s + 1 < NS:
                    emit_wdma(s + 1)
                last = (fc == 7)

                def w1_grp(tg, ex=ex, fc=fc, par=par):
                    pair = (4, 5) if tg % 2 == 0 else (2, 3)
                    hrbase = (tg % 2) * 2048
                    bhr = B_hrp[tg % 2]
                    for f in range(4):
                        bi = pair[f % 2]
                        bank = B_pb[bi]
                        for d in range(8):
                            TE(lambda e, f=f, d=d, tg=tg, bi=bi, par=par:
                               e.matmul(
                                   pb[bi][:, 0:512],
                                   ew1_t[par][:, (f * 8 + d) * 128:
                                              (f * 8 + d + 1) * 128],
                                   mfm_t[:, d * 2048 + tg * 512:
                                         d * 2048 + (tg + 1) * 512],
                                   start=(d == 0), stop=(d == 7)),
                               reads=[B_ew1[par][f], b_mfm], writes=[bank])
                        idx = fc * 4 + f
                        S(lambda e, f=f, idx=idx, ex=ex, bi=bi, hrbase=hrbase:
                          e.activation(
                              hr_t[:, hrbase + f * 512:hrbase + (f + 1) * 512],
                              pb[bi][:, 0:512],
                              AF.Gelu_apprx_tanh,
                              bias=mb1a_t[:, ex * 32 + idx:ex * 32 + idx + 1]),
                          reads=[bank, b_mb1a], writes=[bhr])

                def w2_grp(tg, ex=ex, par=par, last=last):
                    hrbase = (tg % 2) * 2048
                    bhr = B_hrp[tg % 2]
                    for t in range(NB):
                        tt = tg * 4 + t
                        for hf in range(2):
                            cb = ci[0] % 2
                            ci[0] += 1
                            for k in range(4):
                                TE(lambda e, k=k, t=t, hf=hf, cb=cb, par=par,
                                   hrbase=hrbase: e.matmul(
                                       pb[cb][:, 0:512],
                                       hr_t[:, hrbase + k * 512 + t * 128:
                                            hrbase + k * 512 + (t + 1) * 128],
                                       ew2_t[par][:, k * D + hf * 512:
                                                  k * D + (hf + 1) * 512],
                                       start=(k == 0),
                                       stop=(k == 3 and not last)),
                                   reads=[B_ew2[par][k // 2], bhr],
                                   writes=[B_pb[cb]])
                            if last:
                                TE(lambda e, cb=cb, hf=hf, ex=ex: e.matmul(
                                    pb[cb][:, 0:512], ones_bs[0:1, 0:128],
                                    mb2a_t[ex % 2][0:1,
                                                   hf * 512:(hf + 1) * 512],
                                    start=False, stop=True),
                                   reads=[b_ones_b, B_mb2a[ex % 2]],
                                   writes=[B_pb[cb]])
                            V(lambda e, cb=cb, tt=tt, hf=hf, ex=ex:
                              e.scalar_tensor_tensor(
                                  acc_t[:, tt * D + hf * 512:
                                        tt * D + (hf + 1) * 512],
                                  pb[cb][:, 0:512],
                                  gates_t[:, tt * NE + ex:tt * NE + ex + 1],
                                  acc_t[:, tt * D + hf * 512:
                                        tt * D + (hf + 1) * 512],
                                  ALU.mult, ALU.add),
                              reads=[B_pb[cb], b_gates, b_acc],
                              writes=[b_acc])

                w1_grp(0)
                w1_grp(1)
                w2_grp(0)
                w1_grp(2)
                w2_grp(1)
                w1_grp(3)
                w2_grp(2)
                w2_grp(3)
            for t in range(16):
                xb = B_x0[t % 2]
                P.dma("sync", x0_t[:, (t % 2) * D:(t % 2 + 1) * D],
                      x1_d.ap()[t0 + t * 128: t0 + (t + 1) * 128, :],
                      reads=[b_x1d], writes=[xb])
                ln_tile(acc_t[:, t * D:(t + 1) * D], b_acc, D, scr_t[:, 0:D],
                        b_scr)
                V(lambda e, t=t: e.tensor_tensor(
                    scr_t[:, 0:D], scr_t[:, 0:D],
                    x0_t[:, (t % 2) * D:(t % 2 + 1) * D], ALU.add),
                  reads=[xb, b_scr], writes=[b_scr])
                P.dma("sync", y_out.ap()[t0 + t * 128: t0 + (t + 1) * 128, :],
                      scr_t[:, 0:D], reads=[b_scr])

    P.finish()
    P.emit()
    return nc


_NC_CACHE = {}


def _get_nc(stage):
    if stage not in _NC_CACHE:
        nc = bass.Bass("TRN2", target_bir_lowering=False)
        build(nc, stage)
        _NC_CACHE[stage] = nc
    return _NC_CACHE[stage]


def kernel(**inputs):
    nc = _get_nc(STAGE)
    x = np.asarray(inputs["L0_attn_x"])
    names = {
        "attn_ln_w": "L0_attn_ln_w", "attn_ln_b": "L0_attn_ln_b",
        "attn_in_proj_w": "L0_attn_in_proj_w", "attn_conv_w": "L0_attn_conv_w",
        "attn_conv_b": "L0_attn_conv_b", "attn_dt_bias": "L0_attn_dt_bias",
        "attn_A_log": "L0_attn_A_log", "attn_D_skip": "L0_attn_D_skip",
        "attn_rms_w": "L0_attn_rms_w", "attn_out_proj_w": "L0_attn_out_proj_w",
        "attn_mlp_w1": "L0_attn_mlp_w1", "attn_mlp_b1": "L0_attn_mlp_b1",
        "attn_mlp_w2": "L0_attn_mlp_w2", "attn_mlp_b2": "L0_attn_mlp_b2",
        "ffn_ln1_g": "L0_ffn_ln1_g", "ffn_ln1_b": "L0_ffn_ln1_b",
        "ffn_ln2_g": "L0_ffn_ln2_g", "ffn_ln2_b": "L0_ffn_ln2_b",
        "ffn_w_qkv": "L0_ffn_w_qkv", "ffn_w_o": "L0_ffn_w_o",
        "ffn_w_gate": "L0_ffn_w_gate", "ffn_w1": "L0_ffn_w1",
        "ffn_b1": "L0_ffn_b1", "ffn_w2": "L0_ffn_w2", "ffn_b2": "L0_ffn_b2",
    }
    shared = {k: np.ascontiguousarray(np.asarray(inputs[v], dtype=np.float32))
              for k, v in names.items()}
    in_maps = []
    for b in range(8):
        m = dict(shared)
        m["x"] = np.ascontiguousarray(x[b].astype(np.float32))
        in_maps.append(m)
    res = run_bass_kernel_spmd(nc, in_maps, core_ids=list(range(8)))
    return np.stack([r["y"] for r in res.results], axis=0)
